# version 05
# baseline (speedup 1.0000x reference)
import os
import numpy as np
import ml_dtypes
from contextlib import ExitStack

import concourse.bass as bass
import concourse.mybir as mybir
from concourse.bass_utils import run_bass_kernel_spmd

F32 = mybir.dt.float32
BF16 = mybir.dt.bfloat16
I32 = mybir.dt.int32
AF = mybir.ActivationFunctionType
ALU = mybir.AluOpType
AX = mybir.AxisListType

B, T, D = 16, 2048, 1024
NCORE = 8
SEQ_PER_CORE = B // NCORE
ROWS = SEQ_PER_CORE * T
H3 = 3 * D
FFN_H = 3072
N_HEAD = 16
HD = 64
LORA_R = 16
LORA_SCALE = 2.0
LN_EPS = 1e-5
DEN_EPS = 1e-6
TILE = 512
NT = T // TILE
NSL = TILE // 128

_PHASES = int(os.environ.get("MK_PHASES", "4"))
_P1LVL = int(os.environ.get("MK_P1_LEVEL", "5"))
_P1SUB = os.environ.get("MK_P1_SUB", "qgkv")

ENGINES = ("sync", "gpsimd", "vector", "scalar", "tensor")



def ap_cpf(td, nchunk, width, dtw=None):
    return bass.AP(td, 0, [[width, 128], [128 * width, nchunk], [1, width]])


def ap_rows(td, row0, nsl):
    return bass.AP(td, row0 * D, [[D, 128], [128 * D, nsl], [1, D]])

class Tok:
    __slots__ = ("sem", "idx", "step", "used", "val")

    def __init__(self, sem, idx, step):
        self.sem = sem
        self.idx = idx
        self.step = step
        self.used = False
        self.val = None


class Sq:

    def __init__(self, nc, sem_es, name):
        self.nc = nc
        self.es = sem_es
        self.name = name
        self.items = {e: [] for e in ENGINES}
        self.sems = {}
        self.toklist = {}
        self.steps = {}

    def _sem(self, n, step=None):
        if n not in self.sems:
            self.sems[n] = self.es.enter_context(
                self.nc.semaphore(f"{self.name}_{n}"))
            self.toklist[n] = []
            self.steps[n] = step
        if step is not None:
            assert self.steps[n] in (None, step)
            self.steps[n] = step
        return self.sems[n]

    def do(self, eng, emit, wait=(), inc=None, dma=False):
        tok = None
        step = 16 if dma else 1
        wait = [w for w in wait if w is not None]
        if inc is not None:
            self._sem(inc, step)
            tl = self.toklist[inc]
            if dma and tl:
                wait = wait + [tl[-1]]
            tok = Tok(inc, len(tl), step)
            tl.append(tok)
        for w in wait:
            w.used = True
        self.items[eng].append((wait, emit, tok))
        return tok

    def cur(self, semname):
        tl = self.toklist.get(semname)
        if not tl:
            return None
        return tl[-1]

    def _assign(self):
        for semname, tl in self.toklist.items():
            c = 0
            for t in tl:
                if t.used:
                    c += t.step
                t.val = c

    def build(self):
        self._assign()
        with self.nc.Block() as block:
            for engname in ENGINES:
                items = self.items[engname]
                if not items:
                    continue

                def mk(items):
                    def f(eng):
                        known = {}
                        for waits, emit, tok in items:
                            wm = {}
                            for w in waits:
                                v = w.val
                                if v and wm.get(w.sem, -1) < v:
                                    wm[w.sem] = v
                            for smn in sorted(wm):
                                v = wm[smn]
                                if known.get(smn, -1) < v:
                                    eng.wait_ge(self.sems[smn], v)
                                    known[smn] = v
                            ins = emit(eng)
                            if tok is not None and tok.used:
                                assert ins is not None
                                ins.then_inc(self.sems[tok.sem], tok.step)
                    return f
                getattr(block, engname)(mk(items))


class Buf:

    def __init__(self):
        self.w = {}
        self.r = {}

    @staticmethod
    def _add(d, tok):
        if tok is None:
            return
        old = d.get(tok.sem)
        if old is None or old.idx < tok.idx:
            d[tok.sem] = tok

    def war(self):
        out = dict(self.r)
        for sname, t in self.w.items():
            old = out.get(sname)
            if old is None or old.idx < t.idx:
                out[sname] = t
        return list(out.values())

    def raw(self):
        return list(self.w.values())

    def wrote(self, tok):
        self.w = {}
        self._add(self.w, tok)
        self.r = {}

    def wrote_more(self, tok):
        self._add(self.w, tok)

    def read(self, tok):
        self._add(self.r, tok)


def emit_ln(sq, st, sqscr, src_ap, dst_ap, col, src_raw, st_war, dst_war,
            eps_ln):
    c0 = col * 8
    t_sq = sq.do("scalar", lambda e: e.activation(
        sqscr, src_ap, AF.Square, accum_out=st[:, c0 + 1:c0 + 2]),
        wait=list(src_raw) + list(st_war) + list(dst_war), inc="sc")
    t_sum = sq.do("vector", lambda e: e.tensor_reduce(
        st[:, c0:c0 + 1], src_ap, AX.X, ALU.add),
        wait=list(src_raw) + list(st_war), inc="dv")
    t_mu = sq.do("vector", lambda e: e.tensor_scalar(
        st[:, c0 + 2:c0 + 3], st[:, c0:c0 + 1], 1.0 / D, None, ALU.mult),
        wait=[t_sum], inc="dv")
    t_msq = sq.do("vector", lambda e: e.tensor_mul(
        st[:, c0 + 3:c0 + 4], st[:, c0 + 2:c0 + 3], st[:, c0 + 2:c0 + 3]),
        wait=[t_mu], inc="dv")
    t_ex2 = sq.do("vector", lambda e: e.tensor_scalar(
        st[:, c0 + 4:c0 + 5], st[:, c0 + 1:c0 + 2], 1.0 / D, None, ALU.mult),
        wait=[t_sq], inc="dv")
    t_var = sq.do("vector", lambda e: e.tensor_sub(
        st[:, c0 + 4:c0 + 5], st[:, c0 + 4:c0 + 5], st[:, c0 + 3:c0 + 4]),
        wait=[t_msq, t_ex2], inc="dv")
    t_ln = sq.do("scalar", lambda e: e.activation(
        st[:, c0 + 5:c0 + 6], st[:, c0 + 4:c0 + 5], AF.Ln, bias=eps_ln[:]),
        wait=[t_var], inc="sc")
    t_rstd = sq.do("scalar", lambda e: e.activation(
        st[:, c0 + 6:c0 + 7], st[:, c0 + 5:c0 + 6], AF.Exp, scale=-0.5),
        wait=[t_ln], inc="sc")
    t_out = sq.do("vector", lambda e: e.tensor_scalar(
        dst_ap, src_ap, st[:, c0 + 2:c0 + 3], st[:, c0 + 6:c0 + 7],
        ALU.subtract, ALU.mult),
        wait=[t_rstd] + list(dst_war), inc="dv")
    return t_sq, t_out


def build_program():
    nc = bass.Bass("TRN2", target_bir_lowering=False)
    g_es = ExitStack()

    x_d = nc.dram_tensor("x", [ROWS, D], F32, kind="ExternalInput")
    out_d = nc.dram_tensor("out", [ROWS, D], F32, kind="ExternalOutput")

    qkvw_d = nc.dram_tensor("qkv_w", [D, H3], F32, kind="ExternalInput")
    qkvb_d = nc.dram_tensor("qkv_b", [1, H3], F32, kind="ExternalInput")
    qkvbpc_d = nc.dram_tensor("qkv_b_pc", [128, 24], F32, kind="ExternalInput")
    gatebpc_d = nc.dram_tensor("gate_b_pc", [128, 8], F32, kind="ExternalInput")
    gatew_d = nc.dram_tensor("gate_w", [D, D], F32, kind="ExternalInput")
    gateb_d = nc.dram_tensor("gate_b", [1, D], F32, kind="ExternalInput")
    lng_d = nc.dram_tensor("ln_g", [128, 8], F32, kind="ExternalInput")
    lnb_d = nc.dram_tensor("ln_b", [128, 8], F32, kind="ExternalInput")

    wattn_d = nc.dram_tensor("w_attn", [D, H3], F32, kind="ExternalInput")
    battn_d = nc.dram_tensor("b_attn", [1, D], F32, kind="ExternalInput")
    battnpc_d = nc.dram_tensor("b_attn_pc", [128, 16], F32, kind="ExternalInput")
    laattn_d = nc.dram_tensor("la_attn", [D, LORA_R], F32, kind="ExternalInput")
    lbattn_d = nc.dram_tensor("lb_attn", [LORA_R, H3], F32, kind="ExternalInput")
    wproj_d = nc.dram_tensor("w_proj", [D, D], F32, kind="ExternalInput")
    bproj_d = nc.dram_tensor("b_proj", [1, D], F32, kind="ExternalInput")
    laproj_d = nc.dram_tensor("la_proj", [D, LORA_R], F32, kind="ExternalInput")
    lbproj_d = nc.dram_tensor("lb_proj", [LORA_R, D], F32, kind="ExternalInput")
    kvs_d = nc.dram_tensor("kv_scale", [128, 1], F32, kind="ExternalInput")
    kvz_d = nc.dram_tensor("kv_zero", [128, 1], F32, kind="ExternalInput")

    wgate_d = nc.dram_tensor("ffn_w_gate", [FFN_H, D], I32, kind="ExternalInput")
    wup_d = nc.dram_tensor("ffn_w_up", [FFN_H, D], I32, kind="ExternalInput")
    wdown_d = nc.dram_tensor("ffn_w_down", [D, FFN_H], I32, kind="ExternalInput")

    ident_d = nc.dram_tensor("ident", [128, 128], BF16, kind="ExternalInput")
    maskt_d = nc.dram_tensor("maskt", [128, 4 * TILE], BF16, kind="ExternalInput")
    sel2_d = nc.dram_tensor("sel2", [2, 256], BF16, kind="ExternalInput")
    ones2_d = nc.dram_tensor("ones2", [128, 2], BF16, kind="ExternalInput")

    x1_d = nc.dram_tensor("x1_s", [ROWS, D], F32)
    x2_d = nc.dram_tensor("x2_s", [ROWS, D], F32)
    x3_d = nc.dram_tensor("x3_s", [ROWS, D], F32)
    wgT_d = nc.dram_tensor("wgT_s", [D, FFN_H], BF16)
    wuT_d = nc.dram_tensor("wuT_s", [D, FFN_H], BF16)
    wdT_d = nc.dram_tensor("wdT_s", [FFN_H, D], BF16)
    qkvwp_d = nc.dram_tensor("qkvwp_s", [D, H3], BF16)
    gatewp_d = nc.dram_tensor("gatewp_s", [D, D], BF16)
    bcol_d = nc.dram_tensor("bcol_s", [128, 32], F32)
    browv_d = nc.dram_tensor("browv_s", [1, D], BF16)

    sb = lambda n, s, d: g_es.enter_context(nc.sbuf_tensor(n, s, d))
    id_s = sb("id_s", [128, 128], BF16)
    maskt_s = sb("maskt_s", [128, 4 * TILE], BF16)
    ones_row = sb("ones_row", [1, TILE], BF16)
    ones_col = sb("ones_col", [128, 1], BF16)
    eps_ln = sb("eps_ln", [128, 1], F32)
    kvs_s = sb("kvs_s", [128, 1], F32)
    kvz_s = sb("kvz_s", [128, 1], F32)
    kvri = sb("kvri", [128, 1], F32)
    kvz5 = sb("kvz5", [128, 1], F32)
    sel2_s = sb("sel2_s", [2, 256], BF16)
    ones2_s = sb("ones2_s", [128, 2], BF16)

    consts = dict(id_s=id_s, maskt_s=maskt_s, ones_row=ones_row,
                  ones_col=ones_col, eps_ln=eps_ln, kvs_s=kvs_s,
                  kvz_s=kvz_s, kvri=kvri, kvz5=kvz5, sel2=sel2_s,
                  ones2=ones2_s)

    es0 = ExitStack()
    sq = Sq(nc, g_es, "p0")
    s0 = lambda n, s, d: es0.enter_context(nc.sbuf_tensor(n, s, d))
    p0 = lambda n, s, d: es0.enter_context(nc.psum_tensor(n, s, d))

    qkvw_bf = s0("p0_qkvw", [128, 8 * H3], BF16)
    gatew_bf = s0("p0_gatew", [128, 8 * D], BF16)
    lnb_s = s0("p0_lnb", [128, 8], F32)
    lng_s = s0("p0_lng", [128, 8], F32)
    lnb_bf = s0("p0_lnbbf", [128, 8], BF16)
    qkvb_s = s0("p0_qkvb", [1, H3], F32)
    gateb_s = s0("p0_gateb", [1, D], F32)
    brow_bf = s0("p0_brow", [1, H3 + D], BF16)
    wbuf = s0("p0_wbuf", [128, 24 * D], BF16)
    wtbuf = s0("p0_wtbuf", [128, 8 * FFN_H], BF16)
    ps_b = p0("p0_psb", [1, TILE], F32)
    ps_t = p0("p0_pst", [128, 1024], BF16)

    for dst, src in ((id_s, ident_d), (maskt_s, maskt_d), (kvs_s, kvs_d),
                     (kvz_s, kvz_d), (lng_s, lng_d), (lnb_s, lnb_d),
                     (qkvb_s, qkvb_d), (gateb_s, gateb_d),
                     (sel2_s, sel2_d), (ones2_s, ones2_d)):
        sq.do("sync", lambda e, dst=dst, src=src: e.dma_start(dst[:], src[:, :]),
              inc="c", dma=True)
    n_c = sq.cur("c")
    sq.do("vector", lambda e: e.memset(ones_row[:], 1.0))
    sq.do("vector", lambda e: e.memset(ones_col[:], 1.0))
    sq.do("vector", lambda e: e.memset(eps_ln[:], LN_EPS))
    sq.do("vector", lambda e: e.reciprocal(kvri[:], kvs_s[:]), wait=[n_c])
    sq.do("vector", lambda e: e.tensor_scalar(
        kvz5[:], kvz_s[:], 0.5, None, ALU.add), wait=[n_c])
    t_lnbbf = sq.do("vector", lambda e: e.tensor_copy(lnb_bf[:], lnb_s[:]),
                    wait=[n_c], inc="dv")

    sq.do("gpsimd", lambda e: e.dma_start(
        qkvw_bf[:], ap_cpf(qkvw_d, 8, H3)),
        inc="wl", dma=True)
    sq.do("gpsimd", lambda e: e.dma_start(
        gatew_bf[:], ap_cpf(gatew_d, 8, D)),
        inc="wl", dma=True)
    n_wl = sq.cur("wl")

    qkvbpc_s = s0("p0_qkvbpc", [128, 24], F32)
    gatebpc_s = s0("p0_gatebpc", [128, 8], F32)
    bcol_sb = s0("p0_bcol", [128, 32], F32)
    browv_sb = s0("p0_browv", [1, D], BF16)
    ps_c = p0("p0_psc", [128, 2], F32)
    sq.do("sync", lambda e: e.dma_start(qkvbpc_s[:], qkvbpc_d[:, :]),
          inc="c", dma=True)
    sq.do("sync", lambda e: e.dma_start(gatebpc_s[:], gatebpc_d[:, :]),
          inc="c", dma=True)
    n_c2 = sq.cur("c")
    psb_buf = Buf()
    t_ms = sq.do("vector", lambda e: e.memset(bcol_sb[:], 0.0), inc="dv")
    t_bcol = None
    jobs = [(c, qkvw_bf, H3, qkvbpc_s, c, c) for c in range(16)] +            [(c, gatew_bf, D, gatebpc_s, c, 24 + c) for c in range(8)]
    for (c, wsb, ws, bsrc, bcidx, dsti) in jobs:
        tokm = None
        for dch in range(8):
            tokm = sq.do("tensor", lambda e, c=c, dch=dch, wsb=wsb, ws=ws:
                         e.matmul(
                             ps_c[:, 0:1], lnb_bf[:, dch:dch + 1].rearrange(
                                 "p a -> p a") if False else wsb[
                                 :, dch * ws + c * 128: dch * ws + (c + 1) * 128],
                             lnb_bf[:, dch:dch + 1],
                             start=(dch == 0), stop=(dch == 7)),
                         wait=([n_wl, t_lnbbf, n_c2] + psb_buf.war())
                         if dch == 0 else (),
                         inc="pe" if dch == 7 else None)
        psb_buf.wrote(tokm)
        t_bcol = sq.do("vector", lambda e, bsrc=bsrc, bcidx=bcidx, dsti=dsti:
                       e.tensor_add(bcol_sb[:, dsti:dsti + 1], ps_c[:, 0:1],
                                    bsrc[:, bcidx:bcidx + 1]),
                       wait=[tokm, t_ms], inc="dv")
        psb_buf.read(t_bcol)
    sq.do("sync", lambda e: e.dma_start(bcol_d[:, :], bcol_sb[:]),
          wait=[t_bcol], inc="st", dma=True)
    t_brow = None
    for f0 in range(2 * D, H3, TILE):
        tokm = None
        for c in range(8):
            tokm = sq.do("tensor", lambda e, c=c, f0=f0:
                         e.matmul(
                             ps_b[:, :], lnb_bf[:, c:c + 1],
                             qkvw_bf[:, c * H3 + f0: c * H3 + f0 + TILE],
                             start=(c == 0), stop=(c == 7)),
                         wait=(psb_buf.war()) if c == 0 else (),
                         inc="pe" if c == 7 else None)
        psb_buf.wrote(tokm)
        t_brow = sq.do("vector", lambda e, f0=f0:
                       e.tensor_add(browv_sb[:, f0 - 2 * D:f0 - 2 * D + TILE],
                                    ps_b[:, :], qkvb_s[:, f0:f0 + TILE]),
                       wait=[tokm], inc="dv")
        psb_buf.read(t_brow)
    tok_g = sq.cur("pe")
    sq.do("sync", lambda e: e.dma_start(browv_d[:, :], browv_sb[:]),
          wait=[t_brow], inc="st", dma=True)
    t_fold = None
    for c in range(8):
        t_fold = sq.do("scalar", lambda e, c=c: e.activation(
            qkvw_bf[:, c * H3:(c + 1) * H3], qkvw_bf[:, c * H3:(c + 1) * H3],
            AF.Identity, scale=lng_s[:, c:c + 1]),
            wait=[tok_g, n_c] if c == 0 else (), inc="sc")
    for c in range(8):
        t_fold = sq.do("scalar", lambda e, c=c: e.activation(
            gatew_bf[:, c * D:(c + 1) * D], gatew_bf[:, c * D:(c + 1) * D],
            AF.Identity, scale=lng_s[:, c:c + 1]), inc="sc")
    sq.do("sync", lambda e: e.dma_start(
        ap_cpf(qkvwp_d, 8, H3), qkvw_bf[:]),
        wait=[t_fold], inc="st", dma=True)
    sq.do("sync", lambda e: e.dma_start(
        ap_cpf(gatewp_d, 8, D), gatew_bf[:]),
        wait=[t_fold], inc="st", dma=True)

    wbuf_b = Buf()
    wtbuf_b = Buf()
    pst_b = Buf()

    def prep_ffn_weight(src_d, dst_d, n_in, n_out):
        in_w = n_out * 128
        ld = sq.do("gpsimd", lambda e: e.dma_start(
            wbuf[:, 0:n_in * in_w],
            ap_cpf(src_d, n_in, in_w)),
            wait=wbuf_b.war(), inc="wl", dma=True)
        wbuf_b.wrote(ld)
        last_ev = None
        for cout in range(n_out):
            for g0 in range(0, n_in, 8):
                gn = min(8, n_in - g0)
                tokt = None
                for gi in range(gn):
                    cin = g0 + gi
                    tokt = sq.do("tensor", lambda e, cin=cin, cout=cout, gi=gi:
                                 e.transpose(
                                     ps_t[:, gi * 128:(gi + 1) * 128],
                                     wbuf[:, cin * in_w + cout * 128:
                                          cin * in_w + cout * 128 + 128],
                                     id_s[:]),
                                 wait=([ld, n_c] + pst_b.war()) if gi == 0 else (),
                                 inc="pe" if gi == gn - 1 else None)
                pst_b.wrote(tokt)
                wbuf_b.read(tokt)
                last_ev = sq.do("scalar", lambda e, cout=cout, g0=g0, gn=gn:
                                e.activation(
                                    wtbuf[:, cout * (n_in * 128) + g0 * 128:
                                          cout * (n_in * 128) + (g0 + gn) * 128],
                                    ps_t[:, 0:gn * 128], AF.Identity),
                                wait=[tokt] + wtbuf_b.war(), inc="sc")
                pst_b.read(last_ev)
                wtbuf_b.wrote_more(last_ev)
        stt = sq.do("sync", lambda e: e.dma_start(
            ap_cpf(dst_d, n_out, n_in * 128),
            wtbuf[:, 0:n_out * n_in * 128]),
            wait=[last_ev], inc="st", dma=True)
        wtbuf_b.read(stt)
        wtbuf_b.w = {}
        return stt

    prep_ffn_weight(wgate_d, wgT_d, 24, 8)
    prep_ffn_weight(wup_d, wuT_d, 24, 8)
    prep_ffn_weight(wdown_d, wdT_d, 8, 24)

    sq.do("sync", lambda e: e.nop(), wait=[sq.cur("st")])
    sq.build()
    es0.close()

    attn_a_phase(nc, g_es, x_d, x1_d if _PHASES >= 2 else out_d,
                 qkvwp_d, gatewp_d, bcol_d, browv_d, consts)
    if _PHASES >= 2:
        ffn_phase(nc, g_es, "p2", x1_d, x2_d if _PHASES >= 3 else out_d,
                  wgT_d, wuT_d, wdT_d, consts)
    if _PHASES >= 3:
        attn_b_phase(nc, g_es, x2_d, x3_d if _PHASES >= 4 else out_d,
                     wattn_d, battn_d, battnpc_d, laattn_d, lbattn_d,
                     wproj_d, bproj_d, laproj_d, lbproj_d, consts)
    if _PHASES >= 4:
        ffn_phase(nc, g_es, "p4", x3_d, out_d, wgT_d, wuT_d, wdT_d, consts)

    g_es.close()
    return nc


def attn_a_phase(nc, g_es, xin_d, xout_d, qkvwp_d, gatewp_d, bcol_d, browv_d, C):
    es = ExitStack()
    sq = Sq(nc, g_es, "p1")
    s_ = lambda n, s, d: es.enter_context(nc.sbuf_tensor("p1_" + n, s, d))
    p_ = lambda n, s, d: es.enter_context(nc.psum_tensor("p1_" + n, s, d))
    id_s, maskt_s = C["id_s"], C["maskt_s"]
    ones_row, ones_col, eps_ln = C["ones_row"], C["ones_col"], C["eps_ln"]

    qkvw = s_("qkvw", [128, 8 * H3], BF16)
    gatew = s_("gatew", [128, 8 * D], BF16)
    bcol = s_("bcol", [128, 32], F32)
    brow_v = s_("browv", [1, D], BF16)
    xt = s_("xt", [128, 2 * NSL * D], F32)
    xn2 = s_("xn2", [128, D], BF16)
    st = s_("st", [128, 8 * 2 * NSL], F32)
    xnT = s_("xnT", [128, 8 * TILE], BF16)
    q_fm = s_("q", [128, 8 * TILE], BF16)
    k_fm = s_("k", [128, 8 * T], BF16)
    v_tm = s_("v", [128, 16 * D], BF16)
    gate_t = s_("gate", [128, TILE], BF16)
    elu_m = s_("elum", [128, TILE], BF16)
    elu_e = s_("elue", [128, TILE], BF16)
    elu_r = s_("elur", [128, TILE], BF16)
    kg = s_("kg", [128, TILE], BF16)
    a_sb = s_("a_sb", [128, 16 * TILE], BF16)
    den = s_("den", [128, NSL], F32)
    drec = s_("drec", [128, NSL], F32)
    yout = xt

    psT = p_("psT", [128, 1024], BF16)
    psQ = p_("psQ", [128, TILE], F32)
    psQ2 = p_("psQ2", [128, TILE], F32)
    psV = p_("psV", [128, 2 * TILE], F32)
    psO = p_("psO", [128, 2 * TILE], F32)
    psD = p_("psD", [128, NSL], F32)

    wtok = sq.do("sync", lambda e: e.dma_start(
        qkvw[:], ap_cpf(qkvwp_d, 8, H3)),
        inc="wl", dma=True)
    wtok2 = sq.do("sync", lambda e: e.dma_start(
        gatew[:], ap_cpf(gatewp_d, 8, D)),
        inc="wl", dma=True)
    btok = sq.do("sync", lambda e: e.dma_start(bcol[:], bcol_d[:, :]),
                 inc="wl", dma=True)
    btok2 = sq.do("sync", lambda e: e.dma_start(brow_v[:], browv_d[:, :]),
                  inc="wl", dma=True)
    wall = [wtok, wtok2, btok, btok2]

    bufs = {n: Buf() for n in
            ["xt0", "xt1", "xn2", "xnT", "q", "k", "v", "gate", "kg",
             "elum", "elue", "elur", "a_sb", "psT", "psQ", "psQ2", "psV", "psO",
             "psD", "den", "drec", "st"]}

    def elu_chain(src_psum_or_sbuf, src_tok, src_buf, dst_ap, dst_war, bias):
        b_m, b_r, b_e = bufs["elum"], bufs["elur"], bufs["elue"]
        if bias is None:
            tm = sq.do("vector", lambda e: e.tensor_scalar(
                elu_m[:], src_psum_or_sbuf, 0.0, None, ALU.min),
                wait=[src_tok] + b_m.war(), inc="dv")
        else:
            tm = sq.do("vector", lambda e: e.tensor_scalar(
                elu_m[:], src_psum_or_sbuf, bias, 0.0, ALU.add, ALU.min),
                wait=[src_tok] + b_m.war(), inc="dv")
        b_m.wrote(tm)
        src_buf.read(tm)
        if bias is None:
            tr = sq.do("vector", lambda e: e.tensor_sub(
                elu_r[:], src_psum_or_sbuf, elu_m[:]),
                wait=[tm] + b_r.war(), inc="dv")
        else:
            tr = sq.do("vector", lambda e: e.scalar_tensor_tensor(
                elu_r[:], src_psum_or_sbuf, bias, elu_m[:],
                ALU.add, ALU.subtract),
                wait=[tm] + b_r.war(), inc="dv")
        b_r.wrote(tr)
        b_m.read(tr)
        src_buf.read(tr)
        te = sq.do("scalar", lambda e: e.activation(
            elu_e[:], elu_m[:], AF.Exp),
            wait=[tm] + b_e.war(), inc="sc")
        b_e.wrote(te)
        b_m.read(te)
        td = sq.do("vector", lambda e: e.tensor_add(dst_ap, elu_r[:], elu_e[:]),
                   wait=[te, tr] + list(dst_war), inc="dv")
        b_e.read(td)
        b_r.read(td)
        return td

    for s in range(SEQ_PER_CORE):
        r0 = s * T
        for j in range(NT):
            jt0 = j * TILE
            jj = (s * NT + j) % 2
            xo = jj * NSL * D
            xbuf = bufs["xt%d" % jj]
            xl = sq.do("sync", lambda e, r0=r0, jt0=jt0, xo=xo: e.dma_start(
                xt[:, xo:xo + NSL * D], ap_rows(xin_d, r0 + jt0, NSL)),
                wait=xbuf.war(), inc="xl", dma=True)
            xbuf.wrote(xl)

            t_xn2 = []
            for i in range(NSL if _P1LVL >= 1 else 0):
                c0 = 8 * i
                src_ap = xt[:, xo + i * D: xo + (i + 1) * D]
                t_sq = sq.do("scalar", lambda e, src_ap=src_ap, c0=c0:
                             e.activation(
                                 xn2[:, 0:D], src_ap, AF.Square,
                                 accum_out=st[:, c0 + 1:c0 + 2]),
                             wait=[xl] + bufs["st"].war() + bufs["xn2"].war(),
                             inc="sc")
                t_sum = sq.do("vector", lambda e, src_ap=src_ap, c0=c0:
                              e.tensor_reduce(
                                  st[:, c0:c0 + 1], src_ap, AX.X, ALU.add),
                              wait=[xl] + bufs["st"].war(), inc="dv")
                t_mu = sq.do("vector", lambda e, c0=c0: e.tensor_scalar(
                    st[:, c0 + 2:c0 + 3], st[:, c0:c0 + 1], 1.0 / D, None,
                    ALU.mult), wait=[t_sum], inc="dv")
                t_msq = sq.do("vector", lambda e, c0=c0: e.tensor_mul(
                    st[:, c0 + 3:c0 + 4], st[:, c0 + 2:c0 + 3],
                    st[:, c0 + 2:c0 + 3]), wait=[t_mu], inc="dv")
                t_ex2 = sq.do("vector", lambda e, c0=c0: e.tensor_scalar(
                    st[:, c0 + 4:c0 + 5], st[:, c0 + 1:c0 + 2], 1.0 / D, None,
                    ALU.mult), wait=[t_sq], inc="dv")
                t_var = sq.do("vector", lambda e, c0=c0: e.tensor_sub(
                    st[:, c0 + 4:c0 + 5], st[:, c0 + 4:c0 + 5],
                    st[:, c0 + 3:c0 + 4]), wait=[t_msq, t_ex2], inc="dv")
                t_ln = sq.do("scalar", lambda e, c0=c0: e.activation(
                    st[:, c0 + 5:c0 + 6], st[:, c0 + 4:c0 + 5], AF.Ln,
                    bias=eps_ln[:]), wait=[t_var], inc="sc")
                t_r1 = sq.do("scalar", lambda e, c0=c0: e.activation(
                    st[:, c0 + 5:c0 + 6], st[:, c0 + 5:c0 + 6], AF.Exp,
                    scale=-0.5), wait=[t_ln], inc="sc")
                t_a = sq.do("vector", lambda e, c0=c0: e.tensor_mul(
                    st[:, c0 + 6:c0 + 7], st[:, c0 + 5:c0 + 6],
                    st[:, c0 + 5:c0 + 6]), wait=[t_r1], inc="dv")
                t_b = sq.do("vector", lambda e, c0=c0: e.tensor_mul(
                    st[:, c0 + 6:c0 + 7], st[:, c0 + 6:c0 + 7],
                    st[:, c0 + 4:c0 + 5]), wait=[t_a], inc="dv")
                t_c = sq.do("scalar", lambda e, c0=c0: e.activation(
                    st[:, c0 + 6:c0 + 7], st[:, c0 + 6:c0 + 7], AF.Ln,
                    bias=eps_ln[:]), wait=[t_b], inc="sc")
                t_d = sq.do("scalar", lambda e, c0=c0: e.activation(
                    st[:, c0 + 6:c0 + 7], st[:, c0 + 6:c0 + 7], AF.Exp,
                    scale=-0.5), wait=[t_c], inc="sc")
                t_e = sq.do("vector", lambda e, c0=c0: e.tensor_mul(
                    st[:, c0 + 6:c0 + 7], st[:, c0 + 6:c0 + 7],
                    st[:, c0 + 5:c0 + 6]), wait=[t_d], inc="dv")
                t2 = sq.do("vector", lambda e, src_ap=src_ap, c0=c0:
                           e.tensor_scalar(
                               xn2[:, 0:D], src_ap,
                               st[:, c0 + 2:c0 + 3], st[:, c0 + 6:c0 + 7],
                               ALU.subtract, ALU.mult),
                           wait=[t_e, t_sq] + bufs["xn2"].war(), inc="dv")
                bufs["xn2"].wrote(t2)
                t_xn2.append(t2)
                tokt = None
                for c in range(8):
                    tokt = sq.do("tensor", lambda e, i=i, c=c: e.transpose(
                        psT[:, c * 128:(c + 1) * 128],
                        xn2[:, c * 128:(c + 1) * 128], id_s[:]),
                        wait=([t2] + bufs["psT"].war()) if c == 0 else (),
                        inc="pe" if c == 7 else None)
                bufs["psT"].wrote(tokt)
                bufs["xn2"].read(tokt)
                ev_last = sq.do("scalar", lambda e, i=i: e.activation(
                    xnT[:].rearrange("p (c t) -> p c t", t=TILE)
                    [:, :, i * 128:(i + 1) * 128],
                    psT[:], AF.Identity),
                    wait=[tokt] + (bufs["xnT"].war() if i == 0 else []),
                    inc="sc")
                bufs["psT"].read(ev_last)
                bufs["xnT"].wrote_more(ev_last)
            t_xnT = sq.cur("sc")

            for c in range(8 if _P1LVL >= 3 else 0):
                tokm = None
                for dch in range(8):
                    tokm = sq.do("tensor", lambda e, c=c, dch=dch: e.matmul(
                        psQ[:], qkvw[:, dch * H3 + c * 128:
                                     dch * H3 + (c + 1) * 128],
                        xnT[:, dch * TILE:(dch + 1) * TILE],
                        start=(dch == 0), stop=(dch == 7)),
                        wait=([t_xnT] + wall + bufs["psQ"].war())
                        if dch == 0 else (),
                        inc="pe" if dch == 7 else None)
                bufs["psQ"].wrote(tokm)
                bufs["xnT"].read(tokm)
                tq = elu_chain(psQ[:], tokm, bufs["psQ"],
                               q_fm[:, c * TILE:(c + 1) * TILE],
                               bufs["q"].war() if c == 0 else [],
                               bcol[:, c:c + 1])
                bufs["q"].wrote_more(tq)

                tokg = None
                for dch in range(8):
                    tokg = sq.do("tensor", lambda e, c=c, dch=dch: e.matmul(
                        psQ2[:], gatew[:, dch * D + c * 128:
                                       dch * D + (c + 1) * 128],
                        xnT[:, dch * TILE:(dch + 1) * TILE],
                        start=(dch == 0), stop=(dch == 7)),
                        wait=(bufs["psQ2"].war()) if dch == 0 else (),
                        inc="pe" if dch == 7 else None)
                bufs["psQ2"].wrote(tokg)
                bufs["xnT"].read(tokg)
                tg = sq.do("scalar", lambda e, c=c: e.activation(
                    gate_t[:], psQ2[:], AF.Sigmoid, bias=bcol[:, 24 + c:25 + c]),
                    wait=[tokg] + bufs["gate"].war(), inc="sc")
                bufs["gate"].wrote(tg)
                bufs["psQ2"].read(tg)

                tokk = None
                for dch in range(8):
                    tokk = sq.do("tensor", lambda e, c=c, dch=dch: e.matmul(
                        psQ2[:], qkvw[:, dch * H3 + D + c * 128:
                                      dch * H3 + D + (c + 1) * 128],
                        xnT[:, dch * TILE:(dch + 1) * TILE],
                        start=(dch == 0), stop=(dch == 7)),
                        wait=(bufs["psQ2"].war()) if dch == 0 else (),
                        inc="pe" if dch == 7 else None)
                bufs["psQ2"].wrote(tokk)
                bufs["xnT"].read(tokk)
                tkg = sq.do("vector", lambda e, c=c: e.scalar_tensor_tensor(
                    kg[:], psQ2[:], bcol[:, 8 + c:9 + c], gate_t[:],
                    ALU.add, ALU.mult),
                    wait=[tokk, tg] + bufs["kg"].war(), inc="dv")
                bufs["kg"].wrote(tkg)
                bufs["psQ2"].read(tkg)
                bufs["gate"].read(tkg)
                tk = elu_chain(kg[:], tkg, bufs["kg"],
                               k_fm[:, c * T + jt0: c * T + jt0 + TILE],
                               bufs["k"].war() if (c == 0 and j == 0) else [],
                               None)
                bufs["k"].wrote_more(tk)
            t_k_done = sq.cur("dv")

            tv = None
            for i in range(NSL if (_P1LVL >= 3 and "v" in _P1SUB) else 0):
                tokv = None
                for half in range(2):
                    f0 = 2 * D + half * TILE
                    sq.do("tensor", lambda e, f0=f0, half=half: e.matmul(
                        psV[:, half * TILE:(half + 1) * TILE],
                        ones_row[:, 0:128], brow_v[:, f0 - 2 * D:f0 - 2 * D + TILE],
                        start=True, stop=False),
                        wait=([t_xnT] + bufs["psV"].war()) if half == 0 else ())
                    for dch in range(8):
                        tokv = sq.do("tensor",
                                     lambda e, i=i, f0=f0, dch=dch, half=half:
                                     e.matmul(
                                         psV[:, half * TILE:(half + 1) * TILE],
                                         xnT[:, dch * TILE + i * 128:
                                             dch * TILE + (i + 1) * 128],
                                         qkvw[:, dch * H3 + f0:
                                              dch * H3 + f0 + TILE],
                                         start=False, stop=(dch == 7)),
                                     inc="pe" if (dch == 7 and half == 1) else None)
                bufs["psV"].wrote(tokv)
                bufs["xnT"].read(tokv)
                tv = sq.do("scalar", lambda e, i=i, jt0=jt0: e.activation(
                    v_tm[:, (jt0 // 128 + i) * D:(jt0 // 128 + i + 1) * D],
                    psV[:], AF.Identity),
                    wait=[tokv] + (bufs["v"].war() if (i == 0 and j == 0) else []),
                    inc="sc")
                bufs["psV"].read(tv)
                bufs["v"].wrote_more(tv)
            t_v_done = tv

            n_stile = (4 * (j + 1)) if _P1LVL >= 4 else 0
            a_toks = []
            for stp in range(n_stile):
                sk0 = stp * 128
                ps_a = psQ if stp % 2 == 0 else psQ2
                pbuf = bufs["psQ" if stp % 2 == 0 else "psQ2"]
                toka = None
                for dch in range(8):
                    toka = sq.do("tensor",
                                 lambda e, sk0=sk0, dch=dch, ps_a=ps_a:
                                 e.matmul(
                        ps_a[:], k_fm[:, dch * T + sk0: dch * T + sk0 + 128],
                        q_fm[:, dch * TILE:(dch + 1) * TILE],
                        start=(dch == 0), stop=(dch == 7)),
                        wait=([t_k_done] + pbuf.war() +
                              (bufs["a_sb"].war() if stp == 0 else []))
                        if dch == 0 else (),
                        inc="pe" if dch == 7 else None)
                pbuf.wrote(toka)
                bufs["k"].read(toka)
                bufs["q"].read(toka)
                if stp >= 4 * j:
                    o = stp - 4 * j
                    ta = sq.do("vector", lambda e, stp=stp, o=o, ps_a=ps_a:
                               e.tensor_mul(
                        a_sb[:, stp * TILE:(stp + 1) * TILE], ps_a[:],
                        maskt_s[:, o * TILE:(o + 1) * TILE]),
                        wait=[toka], inc="dv")
                else:
                    ta = sq.do("scalar", lambda e, stp=stp, ps_a=ps_a:
                               e.activation(
                        a_sb[:, stp * TILE:(stp + 1) * TILE], ps_a[:],
                        AF.Identity),
                        wait=[toka], inc="sc")
                pbuf.read(ta)
                bufs["a_sb"].wrote_more(ta)
                a_toks.append(ta)

            tokd = None
            for i in range(NSL if _P1LVL >= 5 else 0):
                for stp in range(n_stile):
                    tokd = sq.do("tensor", lambda e, i=i, stp=stp, ns=n_stile:
                        e.matmul(
                        psD[:, i:i + 1],
                        a_sb[:, stp * TILE + i * 128: stp * TILE + (i + 1) * 128],
                        ones_col[:, :],
                        start=(stp == 0 and i == 0),
                        stop=(stp == ns - 1 and i == NSL - 1)),
                        wait=([a_toks[stp]] +
                              (bufs["psD"].war() if (i == 0 and stp == 0) else [])),
                        inc="pe" if (stp == n_stile - 1 and i == NSL - 1)
                        else None)
            bufs["psD"].wrote(tokd)
            if _P1LVL < 5:
                ts_ = sq.do("gpsimd", lambda e, r0=r0, jt0=jt0, xo=xo:
                            e.dma_start(
                    ap_rows(xout_d, r0 + jt0, NSL), xt[:, xo:xo + NSL * D]),
                    wait=[xl] + xbuf.war(), inc="st", dma=True)
                xbuf.read(ts_)
                continue
            tden = sq.do("vector", lambda e: e.tensor_scalar(
                den[:, :], psD[:, :], DEN_EPS, None, ALU.add),
                wait=[tokd] + bufs["den"].war(), inc="dv")
            bufs["den"].wrote(tden)
            trc = sq.do("vector", lambda e: e.reciprocal(
                drec[:, :], den[:, :]),
                wait=[tden] + bufs["drec"].war(), inc="dv")
            bufs["den"].read(trc)
            bufs["psD"].read(trc)
            bufs["drec"].wrote(trc)

            ty = None
            for i in range(NSL):
                toko = None
                for stp in range(n_stile):
                    w0 = [a_toks[stp]]
                    if stp == 0:
                        w0 += bufs["psO"].war() + [t_v_done]
                    for half in range(2):
                        toko = sq.do("tensor",
                              lambda e, i=i, stp=stp, half=half, ns=n_stile:
                              e.matmul(
                                  psO[:, half * TILE:(half + 1) * TILE],
                                  a_sb[:, stp * TILE + i * 128:
                                       stp * TILE + (i + 1) * 128],
                                  v_tm[:, stp * D + half * TILE:
                                       stp * D + (half + 1) * TILE],
                                  start=(stp == 0), stop=(stp == ns - 1)),
                              wait=w0 if half == 0 else (),
                              inc="pe" if (stp == n_stile - 1 and half == 1)
                              else None)
                        w0 = []
                bufs["psO"].wrote(toko)
                bufs["a_sb"].read(toko)
                bufs["v"].read(toko)
                for half in range(2):
                    ty = sq.do("vector", lambda e, i=i, half=half, xo=xo:
                               e.scalar_tensor_tensor(
                                   xt[:, xo + i * D + half * TILE:
                                      xo + i * D + half * TILE + TILE],
                                   psO[:, half * TILE:(half + 1) * TILE],
                                   drec[:, i:i + 1],
                                   xt[:, xo + i * D + half * TILE:
                                      xo + i * D + half * TILE + TILE],
                                   ALU.mult, ALU.add),
                               wait=[trc, toko], inc="dv")
                    bufs["psO"].read(ty)
                bufs["drec"].read(ty)
                xbuf.wrote_more(ty)

            ts_ = sq.do("gpsimd", lambda e, r0=r0, jt0=jt0, xo=xo:
                        e.dma_start(
                ap_rows(xout_d, r0 + jt0, NSL), xt[:, xo:xo + NSL * D]),
                wait=[ty], inc="st", dma=True)
            xbuf.read(ts_)

    sq.do("gpsimd", lambda e: e.nop(), wait=[sq.cur("st")])
    sq.build()
    es.close()


def ffn_phase(nc, g_es, name, xin_d, xout_d, wgT_d, wuT_d, wdT_d, C):
    es = ExitStack()
    sq = Sq(nc, g_es, name)
    s_ = lambda n, s, d: es.enter_context(nc.sbuf_tensor(name + "_" + n, s, d))
    p_ = lambda n, s, d: es.enter_context(nc.psum_tensor(name + "_" + n, s, d))
    id_s, eps_ln = C["id_s"], C["eps_ln"]
    R = 1.0 / 127.0

    wgT = s_("wgT", [128, 8 * FFN_H], BF16)
    wuT = s_("wuT", [128, 8 * FFN_H], BF16)
    wdT = s_("wdT", [128, 24 * D], BF16)
    xt = s_("xt", [128, NSL * D], F32)
    xn = s_("xn", [128, D], BF16)
    st = s_("st", [128, 8 * NSL], F32)
    xnT = s_("xnT", [128, 8 * TILE], BF16)
    sil = s_("sil", [128, TILE], BF16)
    h_fm = s_("h", [128, 24 * TILE], BF16)
    xr = s_("xr", [128, 2 * TILE], F32)

    psT = p_("psT", [128, 1024], BF16)
    psG = p_("psG", [128, TILE], F32)
    psU = p_("psU", [128, TILE], F32)
    psY = p_("psY", [128, 2 * TILE], F32)

    w1 = sq.do("sync", lambda e: e.dma_start(wgT[:], ap_cpf(wgT_d, 8, FFN_H)),
               inc="wl", dma=True)
    w2 = sq.do("sync", lambda e: e.dma_start(wuT[:], ap_cpf(wuT_d, 8, FFN_H)),
               inc="wl", dma=True)
    w3 = sq.do("sync", lambda e: e.dma_start(wdT[:], ap_cpf(wdT_d, 24, D)),
               inc="wl", dma=True)
    wall = [w1, w2, w3]

    bufs = {n: Buf() for n in ["xt", "xn", "xnT", "st", "sil", "h",
                               "xr0", "xr1", "psT", "psG", "psU",
                               "psY0", "psY1"]}

    for ti in range(ROWS // TILE):
        r0 = ti * TILE
        xl = sq.do("sync", lambda e, r0=r0: e.dma_start(
            xt[:], ap_rows(xin_d, r0, NSL)),
            wait=bufs["xt"].war(), inc="xl", dma=True)
        bufs["xt"].wrote(xl)

        t_xn = []
        for i in range(NSL):
            src_ap = xt[:, i * D:(i + 1) * D]
            _, t1 = emit_ln(
                sq, st, xn[:, 0:D], src_ap, xn[:, 0:D],
                i, [xl], bufs["st"].war(), bufs["xn"].war(), eps_ln)
            bufs["xn"].wrote(t1)
            bufs["xt"].read(t1)
            t_xn.append(t1)
            tokt = None
            for c in range(8):
                tokt = sq.do("tensor", lambda e, c=c: e.transpose(
                    psT[:, c * 128:(c + 1) * 128],
                    xn[:, c * 128:(c + 1) * 128], id_s[:]),
                    wait=([t1] + bufs["psT"].war()) if c == 0 else (),
                    inc="pe" if c == 7 else None)
            bufs["psT"].wrote(tokt)
            bufs["xn"].read(tokt)
            ev = sq.do("scalar", lambda e, i=i: e.activation(
                xnT[:].rearrange("p (c t) -> p c t", t=TILE)
                [:, :, i * 128:(i + 1) * 128],
                psT[:], AF.Identity),
                wait=[tokt] + (bufs["xnT"].war() if i == 0 else []),
                inc="sc")
            bufs["psT"].read(ev)
            bufs["xnT"].wrote_more(ev)
        t_xnT = sq.cur("sc")

        for hc in range(24):
            tokg = None
            for dch in range(8):
                tokg = sq.do("tensor", lambda e, hc=hc, dch=dch: e.matmul(
                    psG[:], wgT[:, dch * FFN_H + hc * 128:
                                dch * FFN_H + (hc + 1) * 128],
                    xnT[:, dch * TILE:(dch + 1) * TILE],
                    start=(dch == 0), stop=(dch == 7)),
                    wait=([t_xnT] + wall + bufs["psG"].war()) if dch == 0 else (),
                    inc="pe" if dch == 7 else None)
            bufs["psG"].wrote(tokg)
            bufs["xnT"].read(tokg)
            toku = None
            for dch in range(8):
                toku = sq.do("tensor", lambda e, hc=hc, dch=dch: e.matmul(
                    psU[:], wuT[:, dch * FFN_H + hc * 128:
                                dch * FFN_H + (hc + 1) * 128],
                    xnT[:, dch * TILE:(dch + 1) * TILE],
                    start=(dch == 0), stop=(dch == 7)),
                    wait=(bufs["psU"].war()) if dch == 0 else (),
                    inc="pe" if dch == 7 else None)
            bufs["psU"].wrote(toku)
            bufs["xnT"].read(toku)
            tsil = sq.do("scalar", lambda e: e.activation(
                sil[:], psG[:], AF.Silu, scale=R),
                wait=[tokg] + bufs["sil"].war(), inc="sc")
            bufs["sil"].wrote(tsil)
            bufs["psG"].read(tsil)
            th = sq.do("vector", lambda e, hc=hc: e.scalar_tensor_tensor(
                h_fm[:, hc * TILE:(hc + 1) * TILE], psU[:], R, sil[:],
                ALU.mult, ALU.mult),
                wait=[toku, tsil] + (bufs["h"].war() if hc == 0 else []),
                inc="dv")
            bufs["psU"].read(th)
            bufs["sil"].read(th)
            bufs["h"].wrote_more(th)
        t_h = sq.cur("dv")

        for ch in range(2 * NSL):
            i, half = ch // 2, ch % 2
            sl = ch % 2
            xb = bufs["xr%d" % sl]
            tokr = sq.do("gpsimd", lambda e, r0=r0, i=i, half=half, sl=sl:
                         e.dma_start(
                             xr[:, sl * TILE:(sl + 1) * TILE],
                             bass.AP(xin_d, (r0 + i * 128) * D + half * TILE,
                                     [[D, 128], [1, TILE]])),
                         wait=xb.war(), inc="xr", dma=True)
            xb.wrote(tokr)
            yb = bufs["psY%d" % sl]
            toky = None
            for hc in range(24):
                toky = sq.do("tensor", lambda e, i=i, half=half, hc=hc, sl=sl:
                             e.matmul(
                                 psY[:, sl * TILE:(sl + 1) * TILE],
                                 h_fm[:, hc * TILE + i * 128:
                                      hc * TILE + (i + 1) * 128],
                                 wdT[:, hc * D + half * TILE:
                                     hc * D + half * TILE + TILE],
                                 start=(hc == 0), stop=(hc == 23)),
                             wait=([t_h] + yb.war()) if hc == 0 else (),
                             inc="pe" if hc == 23 else None)
            yb.wrote(toky)
            bufs["h"].read(toky)
            ty = sq.do("vector", lambda e, sl=sl: e.scalar_tensor_tensor(
                xr[:, sl * TILE:(sl + 1) * TILE],
                psY[:, sl * TILE:(sl + 1) * TILE], R,
                xr[:, sl * TILE:(sl + 1) * TILE],
                ALU.mult, ALU.add),
                wait=[toky, tokr], inc="dv")
            yb.read(ty)
            xb.wrote_more(ty)
            ts_ = sq.do("gpsimd", lambda e, r0=r0, i=i, half=half, sl=sl:
                        e.dma_start(
                            bass.AP(xout_d, (r0 + i * 128) * D + half * TILE,
                                    [[D, 128], [1, TILE]]),
                            xr[:, sl * TILE:(sl + 1) * TILE]),
                        wait=[ty], inc="st_o", dma=True)
            xb.read(ts_)

    sq.do("gpsimd", lambda e: e.nop(), wait=[sq.cur("st_o")])
    sq.build()
    es.close()


def attn_b_phase(nc, g_es, xin_d, xout_d, wattn_d, battn_d, battnpc_d,
                 laattn_d, lbattn_d,
                 wproj_d, bproj_d, laproj_d, lbproj_d, C):
    es = ExitStack()
    sq = Sq(nc, g_es, "p3")
    s_ = lambda n, s, d: es.enter_context(nc.sbuf_tensor("p3_" + n, s, d))
    p_ = lambda n, s, d: es.enter_context(nc.psum_tensor("p3_" + n, s, d))
    id_s, maskt_s = C["id_s"], C["maskt_s"]
    ones_row, ones_col, eps_ln = C["ones_row"], C["ones_col"], C["eps_ln"]
    kvs_s, kvz_s, kvri, kvz5 = C["kvs_s"], C["kvz_s"], C["kvri"], C["kvz5"]
    sel2 = C["sel2"]

    wattn = s_("wattn", [128, 8 * H3], BF16)
    wproj = s_("wproj", [128, 8 * D], BF16)
    la_sb = s_("la", [128, 8 * LORA_R], BF16)
    lb_sb = s_("lb", [LORA_R, H3], BF16)
    lap_sb = s_("lap", [128, 8 * LORA_R], BF16)
    lbp_sb = s_("lbp", [LORA_R, D], BF16)
    battn = s_("battn", [1, D], BF16)
    battnpc = s_("battnpc", [128, 16], F32)
    bq8 = s_("bq8", [128, 8], F32)
    bkz = s_("bkz", [128, 8], F32)
    bproj = s_("bproj", [1, D], BF16)
    xt = s_("xt", [128, NSL * D], F32)
    xr = s_("xr", [128, 2 * TILE], F32)
    xn = s_("xn", [128, D], BF16)
    st = s_("st", [128, 8 * NSL], F32)
    xnT = s_("xnT", [128, 8 * TILE], BF16)
    lm = s_("lm", [LORA_R, TILE], BF16)
    lmp = s_("lmp", [LORA_R, TILE], BF16)
    q_fm = s_("q", [128, 8 * TILE], BF16)
    k_fm = s_("k", [128, 8 * T], BF16)
    v_tm = s_("v", [128, 16 * D], BF16)
    negc = s_("negc", [128, N_HEAD], F32)
    l1m = s_("l1m", [2, 1], F32)
    l1mb = s_("l1mb", [2, 1], BF16)
    kbneg = s_("kbneg", [128, 1], F32)
    fq1 = s_("fq1", [128, TILE], F32)
    fqi = s_("fqi", [128, TILE], I32)
    fqr = s_("fqr", [128, TILE], BF16)
    fqg = s_("fqg", [128, TILE], BF16)
    absq = fqg
    recip = fq1
    ptA = s_("ptA", [128, 2 * TILE], BF16)
    ptB = s_("ptB", [128, 2 * TILE], BF16)
    out_fm = s_("outfm", [128, 8 * TILE], BF16)
    onesM = s_("onesM", [128, HD], BF16)

    psT = p_("psT", [128, 1024], BF16)
    psA = p_("psA", [128, 2 * TILE], F32)
    psOA = p_("psOA", [128, TILE], F32)
    psOB = p_("psOB", [128, TILE], F32)
    psDA = p_("psDA", [128, TILE], F32)
    psDB = p_("psDB", [128, TILE], F32)
    psY = p_("psY", [128, TILE], F32)
    psLM = psY
    psS = psA

    wl = []
    for dst, src_ap in (
        (wattn, ap_cpf(wattn_d, 8, H3)),
        (wproj, ap_cpf(wproj_d, 8, D)),
        (la_sb, ap_cpf(laattn_d, 8, LORA_R)),
        (lap_sb, ap_cpf(laproj_d, 8, LORA_R)),
    ):
        wl.append(sq.do("gpsimd", lambda e, dst=dst, src_ap=src_ap:
                        e.dma_start(dst[:], src_ap), inc="wl", dma=True))
    for dst, src in ((lb_sb, lbattn_d), (lbp_sb, lbproj_d),
                     (battn, battn_d), (bproj, bproj_d)):
        wl.append(sq.do("gpsimd", lambda e, dst=dst, src=src:
                        e.dma_start(dst[:], src[:, :]), inc="wl", dma=True))
    wl.append(sq.do("sync", lambda e: e.dma_start(
        battnpc[:], battnpc_d[:, :]), inc="wlb", dma=True))
    wall = [wl[-2], wl[-1]]

    sq.do("vector", lambda e: e.memset(onesM[:], 1.0), inc="dv")
    t1 = sq.do("vector", lambda e: e.tensor_scalar(
        kbneg[:], kvz_s[:], -1.0, 255.0, ALU.mult, ALU.add), inc="dv")
    t2 = sq.do("vector", lambda e: e.tensor_max(
        kbneg[:], kbneg[:], kvz_s[:]), wait=[t1], inc="dv")
    t3 = sq.do("vector", lambda e: e.tensor_mul(
        kbneg[:], kbneg[:], kvs_s[:]), wait=[t2], inc="dv")
    t_kb = sq.do("vector", lambda e: e.tensor_scalar(
        kbneg[:], kbneg[:], -1.0, None, ALU.mult), wait=[t3], inc="dv")
    sq.do("vector", lambda e: e.tensor_scalar(
        bq8[:], battnpc[:, 0:8], 0.125, None, ALU.mult),
        wait=[wl[-1]], inc="dv")
    t_bq = sq.do("vector", lambda e: e.tensor_scalar(
        bkz[:], battnpc[:, 8:16], kvri[:], kvz5[:], ALU.mult, ALU.add),
        wait=[wl[-1]], inc="dv")
    t_kb = t_bq

    bufs = {n: Buf() for n in
            ["xt", "xr0", "xr1", "xn", "xnT", "st", "lm", "lmp", "q", "k", "v",
             "negc",
             "l1m", "l1mb", "fq1", "fqi", "fqr", "fqg", "outfm",
             "ptA0", "ptA1", "ptB0", "ptB1",
             "psT", "psA0", "psA1", "psOA", "psOB", "psDA", "psDB", "psY"]}

    def fq_chain(src_ap, src_tok, src_buf, dst_ap, dst_war, zb=None):
        b1, bi, bf, bg = bufs["fq1"], bufs["fqi"], bufs["fqr"], bufs["fqg"]
        zb_ap = kvz5[:] if zb is None else zb
        ta = sq.do("vector", lambda e, zb_ap=zb_ap: e.tensor_scalar(
            fq1[:], src_ap, kvri[:], zb_ap, ALU.mult, ALU.add),
            wait=[src_tok, t_kb] + b1.war(), inc="dv")
        b1.wrote(ta)
        src_buf.read(ta)
        tb = sq.do("vector", lambda e: e.tensor_scalar(
            fq1[:], fq1[:], 0.5, 255.5, ALU.max, ALU.min),
            wait=[ta], inc="dv")
        b1.wrote_more(tb)
        tc = sq.do("scalar", lambda e: e.activation(
            fqi[:], fq1[:], AF.Identity),
            wait=[tb] + bi.war(), inc="sc")
        bi.wrote(tc)
        b1.read(tc)
        td = sq.do("scalar", lambda e: e.activation(
            fqr[:], fqi[:], AF.Identity),
            wait=[tc] + bf.war(), inc="sc")
        bf.wrote(td)
        bi.read(td)
        te = sq.do("vector", lambda e: e.tensor_tensor(
            fqg[:], fqr[:], fq1[:], ALU.is_gt),
            wait=[td] + bg.war(), inc="dv")
        bg.wrote(te)
        b1.read(te)
        bf.read(te)
        tf = sq.do("vector", lambda e: e.tensor_sub(
            fqr[:], fqr[:], fqg[:]), wait=[te], inc="dv")
        bf.wrote_more(tf)
        bg.read(tf)
        tg = sq.do("vector", lambda e: e.tensor_scalar(
            dst_ap, fqr[:], kvz_s[:], kvs_s[:], ALU.subtract, ALU.mult),
            wait=[tf] + list(dst_war), inc="dv")
        bf.read(tg)
        return tg

    for sxi in range(SEQ_PER_CORE):
        r0 = sxi * T
        for j in range(NT):
            jt0 = j * TILE
            xo = 0
            xbuf = bufs["xt"]
            xl = sq.do("sync", lambda e, r0=r0, jt0=jt0: e.dma_start(
                xt[:], ap_rows(xin_d, r0 + jt0, NSL)),
                wait=xbuf.war(), inc="xl", dma=True)
            xbuf.wrote(xl)


            t_xn = []
            for i in range(NSL):
                _, t1n = emit_ln(
                    sq, st, xn[:, 0:D],
                    xt[:, xo + i * D: xo + (i + 1) * D], xn[:, 0:D],
                    i, [xl], bufs["st"].war(), bufs["xn"].war(), eps_ln)
                bufs["xn"].wrote(t1n)
                t_xn.append(t1n)
                tokt = None
                for c in range(8):
                    tokt = sq.do("tensor", lambda e, c=c: e.transpose(
                        psT[:, c * 128:(c + 1) * 128],
                        xn[:, c * 128:(c + 1) * 128], id_s[:]),
                        wait=([t1n] + bufs["psT"].war()) if c == 0 else (),
                        inc="pe" if c == 7 else None)
                bufs["psT"].wrote(tokt)
                bufs["xn"].read(tokt)
                ev = sq.do("scalar", lambda e, i=i: e.activation(
                    xnT[:].rearrange("p (c t) -> p c t", t=TILE)
                    [:, :, i * 128:(i + 1) * 128],
                    psT[:], AF.Identity),
                    wait=[tokt] + (bufs["xnT"].war() if i == 0 else []),
                    inc="sc")
                bufs["psT"].read(ev)
                bufs["xnT"].wrote_more(ev)
            bufs["st"].wrote(None)
            bufs["st"].read(t_xn[-1])
            xbuf.read(t_xn[-1])
            t_xnT = sq.cur("sc")

            tokl = None
            for dch in range(8):
                tokl = sq.do("tensor", lambda e, dch=dch: e.matmul(
                    psLM[0:LORA_R, :], la_sb[:, dch * LORA_R:(dch + 1) * LORA_R],
                    xnT[:, dch * TILE:(dch + 1) * TILE],
                    start=(dch == 0), stop=(dch == 7)),
                    wait=([t_xnT] + wl + bufs["psY"].war()) if dch == 0 else (),
                    inc="pe" if dch == 7 else None)
            bufs["psY"].wrote(tokl)
            bufs["xnT"].read(tokl)
            t_lm = sq.do("scalar", lambda e: e.activation(
                lm[:], psLM[0:LORA_R, :], AF.Identity, scale=LORA_SCALE),
                wait=[tokl] + bufs["lm"].war(), inc="sc")
            bufs["psY"].read(t_lm)
            bufs["lm"].wrote(t_lm)

            for part, base_f in (("q", 0), ("k", D)):
                for c in range(8):
                    f0 = base_f + c * 128
                    for dch in range(8):
                        sq.do("tensor", lambda e, f0=f0, dch=dch: e.matmul(
                            psA[:, 0:TILE],
                            wattn[:, dch * H3 + f0: dch * H3 + f0 + 128],
                            xnT[:, dch * TILE:(dch + 1) * TILE],
                            start=(dch == 0), stop=False),
                            wait=(bufs["psA0"].war()) if dch == 0 else ())
                    tokm = sq.do("tensor", lambda e, f0=f0: e.matmul(
                        psA[:, 0:TILE], lb_sb[:, f0:f0 + 128], lm[:],
                        start=False, stop=True),
                        wait=[t_lm], inc="pe")
                    bufs["psA0"].wrote(tokm)
                    bufs["xnT"].read(tokm)
                    bufs["lm"].read(tokm)
                    if part == "q":
                        tq = sq.do("scalar", lambda e, c=c: e.activation(
                            q_fm[:, c * TILE:(c + 1) * TILE], psA[:, 0:TILE],
                            AF.Identity, scale=0.125, bias=bq8[:, c:c + 1]),
                            wait=[tokm, t_bq] + (bufs["q"].war() if c == 0
                                                 else []),
                            inc="sc")
                        bufs["q"].wrote_more(tq)
                        ta = sq.do("scalar", lambda e, c=c: e.activation(
                            absq[:], psA[:, 0:TILE], AF.Abs, scale=0.125,
                            bias=bq8[:, c:c + 1]),
                            wait=[tokm, t_bq] + bufs["fqg"].war(), inc="sc")
                        bufs["psA0"].read(tq)
                        bufs["psA0"].read(ta)
                        bufs["fqg"].wrote(ta)
                        tl1 = sq.do("tensor", lambda e: e.matmul(
                            psS[0:2, TILE:2 * TILE], C["ones2"][:, :], absq[:],
                            start=True, stop=True),
                            wait=[ta] + bufs["psA1"].war(), inc="pe")
                        bufs["psA1"].wrote(tl1)
                        bufs["fqg"].read(tl1)
                        tmx = sq.do("vector", lambda e: e.tensor_reduce(
                            l1m[:], psS[0:2, TILE:2 * TILE], AX.X, ALU.max),
                            wait=[tl1] + bufs["l1m"].war(), inc="dv")
                        bufs["l1m"].wrote(tmx)
                        bufs["psA1"].read(tmx)
                        tcb = sq.do("vector", lambda e: e.tensor_copy(
                            l1mb[:], l1m[:]),
                            wait=[tmx] + bufs["l1mb"].war(), inc="dv")
                        bufs["l1mb"].wrote(tcb)
                        bufs["l1m"].read(tcb)
                        tb1 = sq.do("tensor", lambda e: e.matmul(
                            psDA[:, 0:1], sel2[:, 0:128], l1mb[:],
                            start=True, stop=False),
                            wait=[tcb] + bufs["psDA"].war())
                        tb2 = sq.do("tensor", lambda e: e.matmul(
                            psDA[:, 1:2], sel2[:, 128:256], l1mb[:],
                            start=False, stop=True),
                            inc="pe")
                        bufs["psDA"].wrote(tb2)
                        bufs["l1mb"].read(tb2)
                        tnc = sq.do("vector", lambda e, c=c: e.tensor_scalar(
                            negc[:, 2 * c:2 * c + 2], psDA[:, 0:2], kbneg[:],
                            None, ALU.mult),
                            wait=[tb2] + (bufs["negc"].war() if c == 0 else []),
                            inc="dv")
                        bufs["psDA"].read(tnc)
                        bufs["negc"].wrote_more(tnc)
                    else:
                        tk = fq_chain(psA[:, 0:TILE], tokm, bufs["psA0"],
                                      k_fm[:, c * T + jt0: c * T + jt0 + TILE],
                                      bufs["k"].war() if (c == 0 and j == 0)
                                      else [], zb=bkz[:, c:c + 1])
                        bufs["k"].wrote_more(tk)

            for i in range(NSL):
                for half in range(2):
                    f0 = 2 * D + half * TILE
                    sq.do("tensor", lambda e, f0=f0, half=half: e.matmul(
                        psA[:, half * TILE:(half + 1) * TILE],
                        ones_row[:, 0:128],
                        battn[:, f0 - 2 * D:f0 - 2 * D + TILE],
                        start=True, stop=False),
                        wait=(bufs["psA0"].war() + bufs["psA1"].war())
                        if half == 0 else ())
                    for dch in range(8):
                        sq.do("tensor",
                              lambda e, i=i, f0=f0, dch=dch, half=half:
                              e.matmul(
                                  psA[:, half * TILE:(half + 1) * TILE],
                                  xnT[:, dch * TILE + i * 128:
                                      dch * TILE + (i + 1) * 128],
                                  wattn[:, dch * H3 + f0: dch * H3 + f0 + TILE],
                                  start=False, stop=False))
                    tokv = sq.do("tensor",
                                 lambda e, i=i, f0=f0, half=half: e.matmul(
                                     psA[:, half * TILE:(half + 1) * TILE],
                                     lm[:, i * 128:(i + 1) * 128],
                                     lb_sb[:, f0:f0 + TILE],
                                     start=False, stop=True),
                                 inc="pe")
                bufs["psA0"].wrote(tokv)
                bufs["psA1"].wrote(tokv)
                bufs["xnT"].read(tokv)
                bufs["lm"].read(tokv)
                for half in range(2):
                    tvq = fq_chain(
                        psA[:, half * TILE:(half + 1) * TILE], tokv,
                        bufs["psA0" if half == 0 else "psA1"],
                        v_tm[:, (jt0 // 128 + i) * D + half * TILE:
                             (jt0 // 128 + i) * D + (half + 1) * TILE],
                        bufs["v"].war() if (i == 0 and j == 0 and half == 0)
                        else [])
                    bufs["v"].wrote_more(tvq)
            t_v_done = sq.cur("dv")

            n_stile = 4 * (j + 1)
            for pair in range(8):
                hA, hB = 2 * pair, 2 * pair + 1
                last_av = None
                ptoks = {}

                def emit_S(stp):
                    pb = stp % 2
                    for (hh, off, pt, key, psS_h, psSb) in (
                            (hA, 0, ptA, "ptA", psA, "psA0"),
                            (hB, 64, ptB, "ptB", psA, "psA1")):
                        sl = 0 if psSb == "psA0" else TILE
                        tS = sq.do("tensor",
                                   lambda e, off=off, stp=stp, sl=sl,
                                   pair=pair: e.matmul(
                                       psA[:, sl:sl + TILE],
                                       k_fm[off:off + 64,
                                            pair * T + stp * 128:
                                            pair * T + stp * 128 + 128],
                                       q_fm[off:off + 64,
                                            pair * TILE:(pair + 1) * TILE],
                                       start=True, stop=True),
                                   wait=([sq.cur("dv"), sq.cur("sc")]
                                         if (stp == 0 and pair == 0) else []) +
                                   bufs[psSb].war(),
                                   inc="pe")
                        bufs[psSb].wrote(tS)
                        bufs["k"].read(tS)
                        bufs["q"].read(tS)
                        pb2 = stp % 2
                        tE = sq.do("scalar",
                                   lambda e, hh=hh, pt=pt, pb=pb2, sl=sl:
                                   e.activation(
                                       pt[:, pb * TILE:(pb + 1) * TILE],
                                       psA[:, sl:sl + TILE], AF.Exp,
                                       bias=negc[:, hh:hh + 1]),
                                   wait=[tS] + bufs[key + str(pb2)].war(),
                                   inc="sc")
                        bufs[psSb].read(tE)
                        bufs["negc"].read(tE)
                        bufs[key + str(pb2)].wrote(tE)
                        tlast = tE
                        if stp >= 4 * j:
                            o = stp - 4 * j
                            tlast = sq.do("vector",
                                          lambda e, pt=pt, pb=pb2, o=o:
                                          e.tensor_mul(
                                              pt[:, pb * TILE:(pb + 1) * TILE],
                                              pt[:, pb * TILE:(pb + 1) * TILE],
                                              maskt_s[:, o * TILE:(o + 1) * TILE]),
                                          wait=[tE], inc="dv")
                            bufs[key + str(pb2)].wrote_more(tlast)
                        ptoks[(hh, stp)] = tlast

                def emit_AV(stp):
                    nonlocal last_av
                    pb = stp % 2
                    wv = [ptoks[(hA, stp)], ptoks[(hB, stp)], t_v_done]
                    if stp == 0:
                        wv += (bufs["psOA"].war() + bufs["psOB"].war() +
                               bufs["psDA"].war() + bufs["psDB"].war())
                    for (hh, off, pt, key, tp, psO_h, psD_h) in (
                            (hA, 0, ptA, "ptA", (0, 0), psOA, psDA),
                            (hB, 64, ptB, "ptB", (0, 64), psOB, psDB)):
                        sq.do("tensor",
                              lambda e, hh=hh, off=off, pt=pt, pb=pb,
                              stp=stp, ns=n_stile, tp=tp, psO_h=psO_h:
                              e.matmul(
                                  psO_h[off:off + 64, :],
                                  v_tm[:, stp * D + hh * HD:
                                       stp * D + hh * HD + HD],
                                  pt[:, pb * TILE:(pb + 1) * TILE],
                                  start=(stp == 0),
                                  stop=(stp == ns - 1),
                                  tile_position=tp),
                              wait=wv)
                        wv.clear()
                        tDN = sq.do("tensor",
                                    lambda e, off=off, pt=pt, pb=pb,
                                    stp=stp, ns=n_stile, tp=tp, psD_h=psD_h:
                                    e.matmul(
                                        psD_h[off:off + 64, :],
                                        onesM[:, :],
                                        pt[:, pb * TILE:(pb + 1) * TILE],
                                        start=(stp == 0),
                                        stop=(stp == ns - 1),
                                        tile_position=tp),
                                    inc="pe" if off == 64 else None)
                        if tDN is not None:
                            bufs["ptA" + str(pb)].read(tDN)
                            bufs["ptB" + str(pb)].read(tDN)
                            last_av = tDN

                emit_S(0)
                for stp in range(1, n_stile):
                    emit_S(stp)
                    emit_AV(stp - 1)
                emit_AV(n_stile - 1)
                bufs["psOA"].wrote(last_av)
                bufs["psOB"].wrote(last_av)
                bufs["psDA"].wrote(last_av)
                bufs["psDB"].wrote(last_av)
                tln = sq.do("scalar", lambda e: e.activation(
                    recip[0:64, :], psDA[0:64, :], AF.Ln),
                    wait=[last_av] + bufs["fq1"].war(), inc="sc")
                tln2 = sq.do("scalar", lambda e: e.activation(
                    recip[64:128, :], psDB[64:128, :], AF.Ln),
                    inc="sc")
                trc = sq.do("scalar", lambda e: e.activation(
                    recip[:], recip[:], AF.Exp, scale=-1.0),
                    wait=[tln2], inc="sc")
                bufs["fq1"].wrote(trc)
                bufs["psDA"].read(trc)
                bufs["psDB"].read(trc)
                to1 = sq.do("vector", lambda e, pair=pair: e.tensor_mul(
                    out_fm[0:64, pair * TILE:(pair + 1) * TILE],
                    psOA[0:64, :], recip[0:64, :]),
                    wait=[trc] + (bufs["outfm"].war() if pair == 0 else []),
                    inc="dv")
                to2 = sq.do("vector", lambda e, pair=pair: e.tensor_mul(
                    out_fm[64:128, pair * TILE:(pair + 1) * TILE],
                    psOB[64:128, :], recip[64:128, :]),
                    wait=[trc], inc="dv")
                bufs["psOA"].read(to1)
                bufs["psOB"].read(to2)
                bufs["fq1"].read(to2)
                bufs["outfm"].wrote_more(to2)
            t_attn = sq.cur("dv")

            tokl2 = None
            for cc in range(8):
                tokl2 = sq.do("tensor", lambda e, cc=cc: e.matmul(
                    psLM[0:LORA_R, :], lap_sb[:, cc * LORA_R:(cc + 1) * LORA_R],
                    out_fm[:, cc * TILE:(cc + 1) * TILE],
                    start=(cc == 0), stop=(cc == 7)),
                    wait=([t_attn] + bufs["psY"].war()) if cc == 0 else (),
                    inc="pe" if cc == 7 else None)
            bufs["psY"].wrote(tokl2)
            bufs["outfm"].read(tokl2)
            t_lmp = sq.do("scalar", lambda e: e.activation(
                lmp[:], psLM[0:LORA_R, :], AF.Identity, scale=LORA_SCALE),
                wait=[tokl2] + bufs["lmp"].war(), inc="sc")
            bufs["psY"].read(t_lmp)
            bufs["lmp"].wrote(t_lmp)

            ty = None
            for i in range(NSL):
                for half in range(2):
                    f0 = half * TILE
                    ch = 2 * i + half
                    sl = ch % 2
                    xb2 = bufs["xr%d" % sl]
                    tokr = sq.do("gpsimd",
                                 lambda e, r0=r0, jt0=jt0, i=i, half=half,
                                 sl=sl: e.dma_start(
                                     xr[:, sl * TILE:(sl + 1) * TILE],
                                     bass.AP(xin_d,
                                             (r0 + jt0 + i * 128) * D +
                                             half * TILE,
                                             [[D, 128], [1, TILE]])),
                                 wait=xb2.war(), inc="xr", dma=True)
                    xb2.wrote(tokr)
                    sq.do("tensor", lambda e, f0=f0: e.matmul(
                        psY[:], ones_row[:, 0:128], bproj[:, f0:f0 + TILE],
                        start=True, stop=False),
                        wait=[t_attn] + bufs["psY"].war())
                    for cc in range(8):
                        sq.do("tensor", lambda e, i=i, f0=f0, cc=cc: e.matmul(
                            psY[:],
                            out_fm[:, cc * TILE + i * 128:
                                   cc * TILE + (i + 1) * 128],
                            wproj[:, cc * D + f0: cc * D + f0 + TILE],
                            start=False, stop=False))
                    toky = sq.do("tensor", lambda e, i=i, f0=f0: e.matmul(
                        psY[:], lmp[:, i * 128:(i + 1) * 128],
                        lbp_sb[:, f0:f0 + TILE],
                        start=False, stop=True),
                        wait=[t_lmp], inc="pe")
                    bufs["psY"].wrote(toky)
                    bufs["outfm"].read(toky)
                    bufs["lmp"].read(toky)
                    ty = sq.do("vector", lambda e, sl=sl:
                               e.tensor_add(
                                   xr[:, sl * TILE:(sl + 1) * TILE],
                                   psY[:],
                                   xr[:, sl * TILE:(sl + 1) * TILE]),
                               wait=[toky, tokr], inc="dv")
                    bufs["psY"].read(ty)
                    xb2.wrote_more(ty)
                    ts_ = sq.do("gpsimd",
                                lambda e, r0=r0, jt0=jt0, i=i, half=half,
                                sl=sl: e.dma_start(
                                    bass.AP(xout_d,
                                            (r0 + jt0 + i * 128) * D +
                                            half * TILE,
                                            [[D, 128], [1, TILE]]),
                                    xr[:, sl * TILE:(sl + 1) * TILE]),
                                wait=[ty], inc="st_o", dma=True)
                    xb2.read(ts_)

    sq.do("gpsimd", lambda e: e.nop(), wait=[sq.cur("st_o")])
    sq.build()
    es.close()


def _copy_phase(nc, g_es, name, xin_d, xout_d):
    es = ExitStack()
    sq = Sq(nc, g_es, name)
    buf = es.enter_context(nc.sbuf_tensor(name + "_buf", [128, 2 * 4096], F32))
    b0, b1 = Buf(), Buf()
    for i in range(ROWS // 128):
        bb = b0 if i % 2 == 0 else b1
        ld = sq.do("sync", lambda e, i=i: e.dma_start(
            buf[:, (i % 2) * 4096:(i % 2) * 4096 + 1024],
            xin_d[i * 128:(i + 1) * 128, :]),
            wait=bb.war(), inc="ld", dma=True)
        bb.wrote(ld)
        st = sq.do("gpsimd", lambda e, i=i: e.dma_start(
            xout_d[i * 128:(i + 1) * 128, :],
            buf[:, (i % 2) * 4096:(i % 2) * 4096 + 1024]),
            wait=[ld], inc="st", dma=True)
        bb.read(st)
    sq.do("gpsimd", lambda e: e.nop(), wait=[sq.cur("st")])
    sq.build()
    es.close()


_NC_CACHE = {}


def _get_program():
    key = _PHASES
    if key not in _NC_CACHE:
        _NC_CACHE[key] = build_program()
    return _NC_CACHE[key]


def _host_consts():
    ident = np.eye(128, dtype=ml_dtypes.bfloat16)
    m = np.zeros((128, 4 * TILE), dtype=ml_dtypes.bfloat16)
    t = np.arange(TILE)
    p = np.arange(128)[:, None]
    for oi, o in enumerate((0, 128, 256, 384)):
        m[:, oi * TILE:(oi + 1) * TILE] = (t[None, :] >= (o + p)).astype(
            ml_dtypes.bfloat16)
    sel2 = np.zeros((2, 256), dtype=ml_dtypes.bfloat16)
    sel2[0, 0:128] = 1
    sel2[1, 128:256] = 1
    ones2 = np.zeros((128, 2), dtype=ml_dtypes.bfloat16)
    ones2[0:64, 0] = 1
    ones2[64:128, 1] = 1
    return ident, m, sel2, ones2


def kernel(**inp):
    nc = _get_program()
    ident, maskt, sel2, ones2 = _host_consts()

    x = np.ascontiguousarray(inp["L0_attn_a_x"]).reshape(B * T, D)

    def row(v):
        return np.ascontiguousarray(np.asarray(v, np.float32).reshape(1, -1))

    def pcol(v, c):
        return np.ascontiguousarray(np.asarray(v, np.float32).reshape(c, 128).T)

    shared = dict(
        qkv_w=np.ascontiguousarray(inp["L0_attn_a_qkv_w"], dtype=np.float32),
        qkv_b=row(inp["L0_attn_a_qkv_b"]),
        qkv_b_pc=pcol(inp["L0_attn_a_qkv_b"], 24),
        gate_b_pc=pcol(inp["L0_attn_a_gate_b"], 8),
        gate_w=np.ascontiguousarray(inp["L0_attn_a_gate_w"], dtype=np.float32),
        gate_b=row(inp["L0_attn_a_gate_b"]),
        ln_g=pcol(inp["L0_attn_a_ln_g"], 8),
        ln_b=pcol(inp["L0_attn_a_ln_b"], 8),
        w_attn=np.ascontiguousarray(inp["L0_attn_b_w_attn"], dtype=np.float32),
        b_attn=row(np.asarray(inp["L0_attn_b_b_attn"], np.float32)[2 * D:3 * D]),
        b_attn_pc=pcol(np.asarray(inp["L0_attn_b_b_attn"], np.float32)[0:2 * D], 16),
        la_attn=np.ascontiguousarray(inp["L0_attn_b_la_attn"], dtype=np.float32),
        lb_attn=np.ascontiguousarray(inp["L0_attn_b_lb_attn"], dtype=np.float32),
        w_proj=np.ascontiguousarray(inp["L0_attn_b_w_proj"], dtype=np.float32),
        b_proj=row(inp["L0_attn_b_b_proj"]),
        la_proj=np.ascontiguousarray(inp["L0_attn_b_la_proj"], dtype=np.float32),
        lb_proj=np.ascontiguousarray(inp["L0_attn_b_lb_proj"], dtype=np.float32),
        kv_scale=np.ascontiguousarray(
            np.broadcast_to(np.asarray(inp["L0_attn_b_kv_scale"],
                                       np.float32).reshape(1, 1), (128, 1))),
        kv_zero=np.ascontiguousarray(
            np.broadcast_to(np.asarray(inp["L0_attn_b_kv_zero_point"],
                                       np.float32).reshape(1, 1), (128, 1))),
        ffn_w_gate=np.ascontiguousarray(inp["L0_ffn_w_gate"], dtype=np.int32),
        ffn_w_up=np.ascontiguousarray(inp["L0_ffn_w_up"], dtype=np.int32),
        ffn_w_down=np.ascontiguousarray(inp["L0_ffn_w_down"], dtype=np.int32),
        ident=ident,
        maskt=maskt,
        sel2=sel2,
        ones2=ones2,
    )

    in_maps = []
    for c in range(NCORE):
        m = dict(shared)
        m["x"] = np.ascontiguousarray(x[c * ROWS:(c + 1) * ROWS])
        in_maps.append(m)

    res = run_bass_kernel_spmd(nc, in_maps, core_ids=list(range(NCORE)))
    out = np.concatenate([r["out"] for r in res.results], axis=0)
    return out.reshape(B, T, D).astype(np.float32)


if __name__ == "__main__":
    print("smoke build only")
    build_program()
    print("build ok")


# version 06
# speedup vs baseline: 1.0029x; 1.0029x over previous
import os
import numpy as np
import ml_dtypes
from contextlib import ExitStack

import concourse.bass as bass
import concourse.mybir as mybir
from concourse.bass_utils import run_bass_kernel_spmd

F32 = mybir.dt.float32
BF16 = mybir.dt.bfloat16
I32 = mybir.dt.int32
AF = mybir.ActivationFunctionType
ALU = mybir.AluOpType
AX = mybir.AxisListType

B, T, D = 16, 2048, 1024
NCORE = 8
SEQ_PER_CORE = B // NCORE
ROWS = SEQ_PER_CORE * T
H3 = 3 * D
FFN_H = 3072
N_HEAD = 16
HD = 64
LORA_R = 16
LORA_SCALE = 2.0
LN_EPS = 1e-5
DEN_EPS = 1e-6
TILE = 512
NT = T // TILE
NSL = TILE // 128

_PHASES = int(os.environ.get("MK_PHASES", "4"))
_P1LVL = int(os.environ.get("MK_P1_LEVEL", "5"))
_P1SUB = os.environ.get("MK_P1_SUB", "qgkv")

ENGINES = ("sync", "gpsimd", "vector", "scalar", "tensor")



def ap_cpf(td, nchunk, width, dtw=None):
    return bass.AP(td, 0, [[width, 128], [128 * width, nchunk], [1, width]])


def ap_rows(td, row0, nsl):
    return bass.AP(td, row0 * D, [[D, 128], [128 * D, nsl], [1, D]])

class Tok:
    __slots__ = ("sem", "idx", "step", "used", "val")

    def __init__(self, sem, idx, step):
        self.sem = sem
        self.idx = idx
        self.step = step
        self.used = False
        self.val = None


class Sq:

    def __init__(self, nc, sem_es, name):
        self.nc = nc
        self.es = sem_es
        self.name = name
        self.items = {e: [] for e in ENGINES}
        self.sems = {}
        self.toklist = {}
        self.steps = {}

    def _sem(self, n, step=None):
        if n not in self.sems:
            self.sems[n] = self.es.enter_context(
                self.nc.semaphore(f"{self.name}_{n}"))
            self.toklist[n] = []
            self.steps[n] = step
        if step is not None:
            assert self.steps[n] in (None, step)
            self.steps[n] = step
        return self.sems[n]

    def do(self, eng, emit, wait=(), inc=None, dma=False):
        tok = None
        step = 16 if dma else 1
        wait = [w for w in wait if w is not None]
        if inc is not None:
            self._sem(inc, step)
            tl = self.toklist[inc]
            if dma and tl:
                wait = wait + [tl[-1]]
            tok = Tok(inc, len(tl), step)
            tl.append(tok)
        for w in wait:
            w.used = True
        self.items[eng].append((wait, emit, tok))
        return tok

    def cur(self, semname):
        tl = self.toklist.get(semname)
        if not tl:
            return None
        return tl[-1]

    def _assign(self):
        for semname, tl in self.toklist.items():
            c = 0
            for t in tl:
                if t.used:
                    c += t.step
                t.val = c

    def build(self):
        self._assign()
        with self.nc.Block() as block:
            for engname in ENGINES:
                items = self.items[engname]
                if not items:
                    continue

                def mk(items):
                    def f(eng):
                        known = {}
                        for waits, emit, tok in items:
                            wm = {}
                            for w in waits:
                                v = w.val
                                if v and wm.get(w.sem, -1) < v:
                                    wm[w.sem] = v
                            for smn in sorted(wm):
                                v = wm[smn]
                                if known.get(smn, -1) < v:
                                    eng.wait_ge(self.sems[smn], v)
                                    known[smn] = v
                            ins = emit(eng)
                            if tok is not None and tok.used:
                                assert ins is not None
                                ins.then_inc(self.sems[tok.sem], tok.step)
                    return f
                getattr(block, engname)(mk(items))


class Buf:

    def __init__(self):
        self.w = {}
        self.r = {}

    @staticmethod
    def _add(d, tok):
        if tok is None:
            return
        old = d.get(tok.sem)
        if old is None or old.idx < tok.idx:
            d[tok.sem] = tok

    def war(self):
        out = dict(self.r)
        for sname, t in self.w.items():
            old = out.get(sname)
            if old is None or old.idx < t.idx:
                out[sname] = t
        return list(out.values())

    def raw(self):
        return list(self.w.values())

    def wrote(self, tok):
        self.w = {}
        self._add(self.w, tok)
        self.r = {}

    def wrote_more(self, tok):
        self._add(self.w, tok)

    def read(self, tok):
        self._add(self.r, tok)


def emit_ln(sq, st, sqscr, src_ap, dst_ap, col, src_raw, st_war, dst_war,
            eps_ln):
    c0 = col * 8
    t_sq = sq.do("scalar", lambda e: e.activation(
        sqscr, src_ap, AF.Square, accum_out=st[:, c0 + 1:c0 + 2]),
        wait=list(src_raw) + list(st_war) + list(dst_war), inc="sc")
    t_sum = sq.do("vector", lambda e: e.tensor_reduce(
        st[:, c0:c0 + 1], src_ap, AX.X, ALU.add),
        wait=list(src_raw) + list(st_war), inc="dv")
    t_mu = sq.do("vector", lambda e: e.tensor_scalar(
        st[:, c0 + 2:c0 + 3], st[:, c0:c0 + 1], 1.0 / D, None, ALU.mult),
        wait=[t_sum], inc="dv")
    t_msq = sq.do("vector", lambda e: e.tensor_mul(
        st[:, c0 + 3:c0 + 4], st[:, c0 + 2:c0 + 3], st[:, c0 + 2:c0 + 3]),
        wait=[t_mu], inc="dv")
    t_ex2 = sq.do("vector", lambda e: e.tensor_scalar(
        st[:, c0 + 4:c0 + 5], st[:, c0 + 1:c0 + 2], 1.0 / D, None, ALU.mult),
        wait=[t_sq], inc="dv")
    t_var = sq.do("vector", lambda e: e.tensor_sub(
        st[:, c0 + 4:c0 + 5], st[:, c0 + 4:c0 + 5], st[:, c0 + 3:c0 + 4]),
        wait=[t_msq, t_ex2], inc="dv")
    t_ln = sq.do("scalar", lambda e: e.activation(
        st[:, c0 + 5:c0 + 6], st[:, c0 + 4:c0 + 5], AF.Ln, bias=eps_ln[:]),
        wait=[t_var], inc="sc")
    t_rstd = sq.do("scalar", lambda e: e.activation(
        st[:, c0 + 6:c0 + 7], st[:, c0 + 5:c0 + 6], AF.Exp, scale=-0.5),
        wait=[t_ln], inc="sc")
    t_out = sq.do("vector", lambda e: e.tensor_scalar(
        dst_ap, src_ap, st[:, c0 + 2:c0 + 3], st[:, c0 + 6:c0 + 7],
        ALU.subtract, ALU.mult),
        wait=[t_rstd] + list(dst_war), inc="dv")
    return t_sq, t_out


def build_program():
    nc = bass.Bass("TRN2", target_bir_lowering=False)
    g_es = ExitStack()

    x_d = nc.dram_tensor("x", [ROWS, D], F32, kind="ExternalInput")
    out_d = nc.dram_tensor("out", [ROWS, D], F32, kind="ExternalOutput")

    qkvw_d = nc.dram_tensor("qkv_w", [D, H3], F32, kind="ExternalInput")
    qkvb_d = nc.dram_tensor("qkv_b", [1, H3], F32, kind="ExternalInput")
    qkvbpc_d = nc.dram_tensor("qkv_b_pc", [128, 24], F32, kind="ExternalInput")
    gatebpc_d = nc.dram_tensor("gate_b_pc", [128, 8], F32, kind="ExternalInput")
    gatew_d = nc.dram_tensor("gate_w", [D, D], F32, kind="ExternalInput")
    gateb_d = nc.dram_tensor("gate_b", [1, D], F32, kind="ExternalInput")
    lng_d = nc.dram_tensor("ln_g", [128, 8], F32, kind="ExternalInput")
    lnb_d = nc.dram_tensor("ln_b", [128, 8], F32, kind="ExternalInput")

    wattn_d = nc.dram_tensor("w_attn", [D, H3], F32, kind="ExternalInput")
    battn_d = nc.dram_tensor("b_attn", [1, D], F32, kind="ExternalInput")
    battnpc_d = nc.dram_tensor("b_attn_pc", [128, 16], F32, kind="ExternalInput")
    laattn_d = nc.dram_tensor("la_attn", [D, LORA_R], F32, kind="ExternalInput")
    lbattn_d = nc.dram_tensor("lb_attn", [LORA_R, H3], F32, kind="ExternalInput")
    wproj_d = nc.dram_tensor("w_proj", [D, D], F32, kind="ExternalInput")
    bproj_d = nc.dram_tensor("b_proj", [1, D], F32, kind="ExternalInput")
    laproj_d = nc.dram_tensor("la_proj", [D, LORA_R], F32, kind="ExternalInput")
    lbproj_d = nc.dram_tensor("lb_proj", [LORA_R, D], F32, kind="ExternalInput")
    kvs_d = nc.dram_tensor("kv_scale", [128, 1], F32, kind="ExternalInput")
    kvz_d = nc.dram_tensor("kv_zero", [128, 1], F32, kind="ExternalInput")

    wgate_d = nc.dram_tensor("ffn_w_gate", [FFN_H, D], I32, kind="ExternalInput")
    wup_d = nc.dram_tensor("ffn_w_up", [FFN_H, D], I32, kind="ExternalInput")
    wdown_d = nc.dram_tensor("ffn_w_down", [D, FFN_H], I32, kind="ExternalInput")

    ident_d = nc.dram_tensor("ident", [128, 128], BF16, kind="ExternalInput")
    maskt_d = nc.dram_tensor("maskt", [128, 4 * TILE], BF16, kind="ExternalInput")
    sel2_d = nc.dram_tensor("sel2", [2, 256], BF16, kind="ExternalInput")
    ones2_d = nc.dram_tensor("ones2", [128, 2], BF16, kind="ExternalInput")

    x1_d = nc.dram_tensor("x1_s", [ROWS, D], F32)
    x2_d = nc.dram_tensor("x2_s", [ROWS, D], F32)
    x3_d = nc.dram_tensor("x3_s", [ROWS, D], F32)
    wgT_d = nc.dram_tensor("wgT_s", [D, FFN_H], BF16)
    wuT_d = nc.dram_tensor("wuT_s", [D, FFN_H], BF16)
    wdT_d = nc.dram_tensor("wdT_s", [FFN_H, D], BF16)
    qkvwp_d = nc.dram_tensor("qkvwp_s", [D, H3], BF16)
    gatewp_d = nc.dram_tensor("gatewp_s", [D, D], BF16)
    bcol_d = nc.dram_tensor("bcol_s", [128, 32], F32)
    browv_d = nc.dram_tensor("browv_s", [1, D], BF16)

    sb = lambda n, s, d: g_es.enter_context(nc.sbuf_tensor(n, s, d))
    id_s = sb("id_s", [128, 128], BF16)
    maskt_s = sb("maskt_s", [128, 4 * TILE], BF16)
    ones_row = sb("ones_row", [1, TILE], BF16)
    ones_col = sb("ones_col", [128, 1], BF16)
    eps_ln = sb("eps_ln", [128, 1], F32)
    kvs_s = sb("kvs_s", [128, 1], F32)
    kvz_s = sb("kvz_s", [128, 1], F32)
    kvri = sb("kvri", [128, 1], F32)
    kvz5 = sb("kvz5", [128, 1], F32)
    sel2_s = sb("sel2_s", [2, 256], BF16)
    ones2_s = sb("ones2_s", [128, 2], BF16)

    consts = dict(id_s=id_s, maskt_s=maskt_s, ones_row=ones_row,
                  ones_col=ones_col, eps_ln=eps_ln, kvs_s=kvs_s,
                  kvz_s=kvz_s, kvri=kvri, kvz5=kvz5, sel2=sel2_s,
                  ones2=ones2_s)

    es0 = ExitStack()
    sq = Sq(nc, g_es, "p0")
    s0 = lambda n, s, d: es0.enter_context(nc.sbuf_tensor(n, s, d))
    p0 = lambda n, s, d: es0.enter_context(nc.psum_tensor(n, s, d))

    qkvw_bf = s0("p0_qkvw", [128, 8 * H3], BF16)
    gatew_bf = s0("p0_gatew", [128, 8 * D], BF16)
    lnb_s = s0("p0_lnb", [128, 8], F32)
    lng_s = s0("p0_lng", [128, 8], F32)
    lnb_bf = s0("p0_lnbbf", [128, 8], BF16)
    qkvb_s = s0("p0_qkvb", [1, H3], F32)
    gateb_s = s0("p0_gateb", [1, D], F32)
    brow_bf = s0("p0_brow", [1, H3 + D], BF16)
    wbuf = s0("p0_wbuf", [128, 24 * D], BF16)
    wtbuf = s0("p0_wtbuf", [128, 8 * FFN_H], BF16)
    ps_b = p0("p0_psb", [1, TILE], F32)
    ps_t = p0("p0_pst", [128, 1024], BF16)

    for dst, src in ((id_s, ident_d), (maskt_s, maskt_d), (kvs_s, kvs_d),
                     (kvz_s, kvz_d), (lng_s, lng_d), (lnb_s, lnb_d),
                     (qkvb_s, qkvb_d), (gateb_s, gateb_d),
                     (sel2_s, sel2_d), (ones2_s, ones2_d)):
        sq.do("sync", lambda e, dst=dst, src=src: e.dma_start(dst[:], src[:, :]),
              inc="c", dma=True)
    n_c = sq.cur("c")
    sq.do("vector", lambda e: e.memset(ones_row[:], 1.0))
    sq.do("vector", lambda e: e.memset(ones_col[:], 1.0))
    sq.do("vector", lambda e: e.memset(eps_ln[:], LN_EPS))
    sq.do("vector", lambda e: e.reciprocal(kvri[:], kvs_s[:]), wait=[n_c])
    sq.do("vector", lambda e: e.tensor_scalar(
        kvz5[:], kvz_s[:], 0.5, None, ALU.add), wait=[n_c])
    t_lnbbf = sq.do("vector", lambda e: e.tensor_copy(lnb_bf[:], lnb_s[:]),
                    wait=[n_c], inc="dv")

    sq.do("gpsimd", lambda e: e.dma_start(
        qkvw_bf[:], ap_cpf(qkvw_d, 8, H3)),
        inc="wl", dma=True)
    sq.do("gpsimd", lambda e: e.dma_start(
        gatew_bf[:], ap_cpf(gatew_d, 8, D)),
        inc="wl", dma=True)
    n_wl = sq.cur("wl")

    qkvbpc_s = s0("p0_qkvbpc", [128, 24], F32)
    gatebpc_s = s0("p0_gatebpc", [128, 8], F32)
    bcol_sb = s0("p0_bcol", [128, 32], F32)
    browv_sb = s0("p0_browv", [1, D], BF16)
    ps_c = p0("p0_psc", [128, 2], F32)
    sq.do("sync", lambda e: e.dma_start(qkvbpc_s[:], qkvbpc_d[:, :]),
          inc="c", dma=True)
    sq.do("sync", lambda e: e.dma_start(gatebpc_s[:], gatebpc_d[:, :]),
          inc="c", dma=True)
    n_c2 = sq.cur("c")
    psb_buf = Buf()
    t_ms = sq.do("vector", lambda e: e.memset(bcol_sb[:], 0.0), inc="dv")
    t_bcol = None
    jobs = [(c, qkvw_bf, H3, qkvbpc_s, c, c) for c in range(16)] +            [(c, gatew_bf, D, gatebpc_s, c, 24 + c) for c in range(8)]
    for (c, wsb, ws, bsrc, bcidx, dsti) in jobs:
        tokm = None
        for dch in range(8):
            tokm = sq.do("tensor", lambda e, c=c, dch=dch, wsb=wsb, ws=ws:
                         e.matmul(
                             ps_c[:, 0:1], lnb_bf[:, dch:dch + 1].rearrange(
                                 "p a -> p a") if False else wsb[
                                 :, dch * ws + c * 128: dch * ws + (c + 1) * 128],
                             lnb_bf[:, dch:dch + 1],
                             start=(dch == 0), stop=(dch == 7)),
                         wait=([n_wl, t_lnbbf, n_c2] + psb_buf.war())
                         if dch == 0 else (),
                         inc="pe" if dch == 7 else None)
        psb_buf.wrote(tokm)
        t_bcol = sq.do("vector", lambda e, bsrc=bsrc, bcidx=bcidx, dsti=dsti:
                       e.tensor_add(bcol_sb[:, dsti:dsti + 1], ps_c[:, 0:1],
                                    bsrc[:, bcidx:bcidx + 1]),
                       wait=[tokm, t_ms], inc="dv")
        psb_buf.read(t_bcol)
    sq.do("sync", lambda e: e.dma_start(bcol_d[:, :], bcol_sb[:]),
          wait=[t_bcol], inc="st", dma=True)
    t_brow = None
    for f0 in range(2 * D, H3, TILE):
        tokm = None
        for c in range(8):
            tokm = sq.do("tensor", lambda e, c=c, f0=f0:
                         e.matmul(
                             ps_b[:, :], lnb_bf[:, c:c + 1],
                             qkvw_bf[:, c * H3 + f0: c * H3 + f0 + TILE],
                             start=(c == 0), stop=(c == 7)),
                         wait=(psb_buf.war()) if c == 0 else (),
                         inc="pe" if c == 7 else None)
        psb_buf.wrote(tokm)
        t_brow = sq.do("vector", lambda e, f0=f0:
                       e.tensor_add(browv_sb[:, f0 - 2 * D:f0 - 2 * D + TILE],
                                    ps_b[:, :], qkvb_s[:, f0:f0 + TILE]),
                       wait=[tokm], inc="dv")
        psb_buf.read(t_brow)
    tok_g = sq.cur("pe")
    sq.do("sync", lambda e: e.dma_start(browv_d[:, :], browv_sb[:]),
          wait=[t_brow], inc="st", dma=True)
    t_fold = None
    for c in range(8):
        t_fold = sq.do("scalar", lambda e, c=c: e.activation(
            qkvw_bf[:, c * H3:(c + 1) * H3], qkvw_bf[:, c * H3:(c + 1) * H3],
            AF.Identity, scale=lng_s[:, c:c + 1]),
            wait=[tok_g, n_c] if c == 0 else (), inc="sc")
    for c in range(8):
        t_fold = sq.do("scalar", lambda e, c=c: e.activation(
            gatew_bf[:, c * D:(c + 1) * D], gatew_bf[:, c * D:(c + 1) * D],
            AF.Identity, scale=lng_s[:, c:c + 1]), inc="sc")
    sq.do("sync", lambda e: e.dma_start(
        ap_cpf(qkvwp_d, 8, H3), qkvw_bf[:]),
        wait=[t_fold], inc="st", dma=True)
    sq.do("sync", lambda e: e.dma_start(
        ap_cpf(gatewp_d, 8, D), gatew_bf[:]),
        wait=[t_fold], inc="st", dma=True)

    wbuf_b = Buf()
    wtbuf_b = Buf()
    pst_b = Buf()

    def prep_ffn_weight(src_d, dst_d, n_in, n_out):
        in_w = n_out * 128
        ld = sq.do("gpsimd", lambda e: e.dma_start(
            wbuf[:, 0:n_in * in_w],
            ap_cpf(src_d, n_in, in_w)),
            wait=wbuf_b.war(), inc="wl", dma=True)
        wbuf_b.wrote(ld)
        last_ev = None
        for cout in range(n_out):
            for g0 in range(0, n_in, 8):
                gn = min(8, n_in - g0)
                tokt = None
                for gi in range(gn):
                    cin = g0 + gi
                    tokt = sq.do("tensor", lambda e, cin=cin, cout=cout, gi=gi:
                                 e.transpose(
                                     ps_t[:, gi * 128:(gi + 1) * 128],
                                     wbuf[:, cin * in_w + cout * 128:
                                          cin * in_w + cout * 128 + 128],
                                     id_s[:]),
                                 wait=([ld, n_c] + pst_b.war()) if gi == 0 else (),
                                 inc="pe" if gi == gn - 1 else None)
                pst_b.wrote(tokt)
                wbuf_b.read(tokt)
                last_ev = sq.do("scalar", lambda e, cout=cout, g0=g0, gn=gn:
                                e.activation(
                                    wtbuf[:, cout * (n_in * 128) + g0 * 128:
                                          cout * (n_in * 128) + (g0 + gn) * 128],
                                    ps_t[:, 0:gn * 128], AF.Identity),
                                wait=[tokt] + wtbuf_b.war(), inc="sc")
                pst_b.read(last_ev)
                wtbuf_b.wrote_more(last_ev)
        stt = sq.do("sync", lambda e: e.dma_start(
            ap_cpf(dst_d, n_out, n_in * 128),
            wtbuf[:, 0:n_out * n_in * 128]),
            wait=[last_ev], inc="st", dma=True)
        wtbuf_b.read(stt)
        wtbuf_b.w = {}
        return stt

    prep_ffn_weight(wgate_d, wgT_d, 24, 8)
    prep_ffn_weight(wup_d, wuT_d, 24, 8)
    prep_ffn_weight(wdown_d, wdT_d, 8, 24)

    sq.do("sync", lambda e: e.nop(), wait=[sq.cur("st")])
    sq.build()
    es0.close()

    attn_a_phase(nc, g_es, x_d, x1_d if _PHASES >= 2 else out_d,
                 qkvwp_d, gatewp_d, bcol_d, browv_d, consts)
    if _PHASES >= 2:
        ffn_phase(nc, g_es, "p2", x1_d, x2_d if _PHASES >= 3 else out_d,
                  wgT_d, wuT_d, wdT_d, consts)
    if _PHASES >= 3:
        attn_b_phase(nc, g_es, x2_d, x3_d if _PHASES >= 4 else out_d,
                     wattn_d, battn_d, battnpc_d, laattn_d, lbattn_d,
                     wproj_d, bproj_d, laproj_d, lbproj_d, consts)
    if _PHASES >= 4:
        ffn_phase(nc, g_es, "p4", x3_d, out_d, wgT_d, wuT_d, wdT_d, consts)

    g_es.close()
    return nc


def attn_a_phase(nc, g_es, xin_d, xout_d, qkvwp_d, gatewp_d, bcol_d, browv_d, C):
    es = ExitStack()
    sq = Sq(nc, g_es, "p1")
    s_ = lambda n, s, d: es.enter_context(nc.sbuf_tensor("p1_" + n, s, d))
    p_ = lambda n, s, d: es.enter_context(nc.psum_tensor("p1_" + n, s, d))
    id_s, maskt_s = C["id_s"], C["maskt_s"]
    ones_row, ones_col, eps_ln = C["ones_row"], C["ones_col"], C["eps_ln"]

    qkvw = s_("qkvw", [128, 8 * H3], BF16)
    gatew = s_("gatew", [128, 8 * D], BF16)
    bcol = s_("bcol", [128, 32], F32)
    brow_v = s_("browv", [1, D], BF16)
    xt = s_("xt", [128, 2 * NSL * D], F32)
    xn2 = s_("xn2", [128, D], BF16)
    st = s_("st", [128, 8 * 2 * NSL], F32)
    xnT = s_("xnT", [128, 8 * TILE], BF16)
    q_fm = s_("q", [128, 8 * TILE], BF16)
    k_fm = s_("k", [128, 8 * T], BF16)
    v_tm = s_("v", [128, 16 * D], BF16)
    gate_t = s_("gate", [128, TILE], BF16)
    elu_m = s_("elum", [128, TILE], BF16)
    elu_e = s_("elue", [128, TILE], BF16)
    elu_r = s_("elur", [128, TILE], BF16)
    kg = s_("kg", [128, TILE], BF16)
    a_sb = s_("a_sb", [128, 16 * TILE], BF16)
    den = s_("den", [128, NSL], F32)
    drec = s_("drec", [128, NSL], F32)
    yout = xt

    psT = p_("psT", [128, 1024], BF16)
    psQ = p_("psQ", [128, TILE], F32)
    psQ2 = p_("psQ2", [128, TILE], F32)
    psV = p_("psV", [128, 2 * TILE], F32)
    psO = p_("psO", [128, 2 * TILE], F32)
    psD = p_("psD", [128, NSL], F32)

    wtok = sq.do("sync", lambda e: e.dma_start(
        qkvw[:], ap_cpf(qkvwp_d, 8, H3)),
        inc="wl", dma=True)
    wtok2 = sq.do("sync", lambda e: e.dma_start(
        gatew[:], ap_cpf(gatewp_d, 8, D)),
        inc="wl", dma=True)
    btok = sq.do("sync", lambda e: e.dma_start(bcol[:], bcol_d[:, :]),
                 inc="wl", dma=True)
    btok2 = sq.do("sync", lambda e: e.dma_start(brow_v[:], browv_d[:, :]),
                  inc="wl", dma=True)
    wall = [wtok, wtok2, btok, btok2]

    bufs = {n: Buf() for n in
            ["xt0", "xt1", "xn2", "xnT", "q", "k", "v", "gate", "kg",
             "elum", "elue", "elur", "a_sb", "psT", "psQ", "psQ2", "psV", "psO",
             "psD", "den", "drec", "st"]}

    def elu_chain(src_psum_or_sbuf, src_tok, src_buf, dst_ap, dst_war, bias):
        b_m, b_r, b_e = bufs["elum"], bufs["elur"], bufs["elue"]
        if bias is None:
            tm = sq.do("vector", lambda e: e.tensor_scalar(
                elu_m[:], src_psum_or_sbuf, 0.0, None, ALU.min),
                wait=[src_tok] + b_m.war(), inc="dv")
        else:
            tm = sq.do("vector", lambda e: e.tensor_scalar(
                elu_m[:], src_psum_or_sbuf, bias, 0.0, ALU.add, ALU.min),
                wait=[src_tok] + b_m.war(), inc="dv")
        b_m.wrote(tm)
        src_buf.read(tm)
        if bias is None:
            tr = sq.do("vector", lambda e: e.tensor_sub(
                elu_r[:], src_psum_or_sbuf, elu_m[:]),
                wait=[tm] + b_r.war(), inc="dv")
        else:
            tr = sq.do("vector", lambda e: e.scalar_tensor_tensor(
                elu_r[:], src_psum_or_sbuf, bias, elu_m[:],
                ALU.add, ALU.subtract),
                wait=[tm] + b_r.war(), inc="dv")
        b_r.wrote(tr)
        b_m.read(tr)
        src_buf.read(tr)
        te = sq.do("scalar", lambda e: e.activation(
            elu_e[:], elu_m[:], AF.Exp),
            wait=[tm] + b_e.war(), inc="sc")
        b_e.wrote(te)
        b_m.read(te)
        td = sq.do("vector", lambda e: e.tensor_add(dst_ap, elu_r[:], elu_e[:]),
                   wait=[te, tr] + list(dst_war), inc="dv")
        b_e.read(td)
        b_r.read(td)
        return td

    def emit_load_lnt(s_i, j):
        r0 = s_i * T
        jt0 = j * TILE
        jj = (s_i * NT + j) % 2
        xo = jj * NSL * D
        xbuf = bufs["xt%d" % jj]
        xl = sq.do("sync", lambda e, r0=r0, jt0=jt0, xo=xo: e.dma_start(
            xt[:, xo:xo + NSL * D], ap_rows(xin_d, r0 + jt0, NSL)),
            wait=xbuf.war(), inc="xl", dma=True)
        xbuf.wrote(xl)


        t_xn2 = []
        for i in range(NSL if _P1LVL >= 1 else 0):
            c0 = 8 * i
            src_ap = xt[:, xo + i * D: xo + (i + 1) * D]
            t_sq = sq.do("scalar", lambda e, src_ap=src_ap, c0=c0:
                         e.activation(
                             xn2[:, 0:D], src_ap, AF.Square,
                             accum_out=st[:, c0 + 1:c0 + 2]),
                         wait=[xl] + bufs["st"].war() + bufs["xn2"].war(),
                         inc="sc")
            t_sum = sq.do("vector", lambda e, src_ap=src_ap, c0=c0:
                          e.tensor_reduce(
                              st[:, c0:c0 + 1], src_ap, AX.X, ALU.add),
                          wait=[xl] + bufs["st"].war(), inc="dv")
            t_mu = sq.do("vector", lambda e, c0=c0: e.tensor_scalar(
                st[:, c0 + 2:c0 + 3], st[:, c0:c0 + 1], 1.0 / D, None,
                ALU.mult), wait=[t_sum], inc="dv")
            t_msq = sq.do("vector", lambda e, c0=c0: e.tensor_mul(
                st[:, c0 + 3:c0 + 4], st[:, c0 + 2:c0 + 3],
                st[:, c0 + 2:c0 + 3]), wait=[t_mu], inc="dv")
            t_ex2 = sq.do("vector", lambda e, c0=c0: e.tensor_scalar(
                st[:, c0 + 4:c0 + 5], st[:, c0 + 1:c0 + 2], 1.0 / D, None,
                ALU.mult), wait=[t_sq], inc="dv")
            t_var = sq.do("vector", lambda e, c0=c0: e.tensor_sub(
                st[:, c0 + 4:c0 + 5], st[:, c0 + 4:c0 + 5],
                st[:, c0 + 3:c0 + 4]), wait=[t_msq, t_ex2], inc="dv")
            t_ln = sq.do("scalar", lambda e, c0=c0: e.activation(
                st[:, c0 + 5:c0 + 6], st[:, c0 + 4:c0 + 5], AF.Ln,
                bias=eps_ln[:]), wait=[t_var], inc="sc")
            t_r1 = sq.do("scalar", lambda e, c0=c0: e.activation(
                st[:, c0 + 5:c0 + 6], st[:, c0 + 5:c0 + 6], AF.Exp,
                scale=-0.5), wait=[t_ln], inc="sc")
            t_a = sq.do("vector", lambda e, c0=c0: e.tensor_mul(
                st[:, c0 + 6:c0 + 7], st[:, c0 + 5:c0 + 6],
                st[:, c0 + 5:c0 + 6]), wait=[t_r1], inc="dv")
            t_b = sq.do("vector", lambda e, c0=c0: e.tensor_mul(
                st[:, c0 + 6:c0 + 7], st[:, c0 + 6:c0 + 7],
                st[:, c0 + 4:c0 + 5]), wait=[t_a], inc="dv")
            t_c = sq.do("scalar", lambda e, c0=c0: e.activation(
                st[:, c0 + 6:c0 + 7], st[:, c0 + 6:c0 + 7], AF.Ln,
                bias=eps_ln[:]), wait=[t_b], inc="sc")
            t_d = sq.do("scalar", lambda e, c0=c0: e.activation(
                st[:, c0 + 6:c0 + 7], st[:, c0 + 6:c0 + 7], AF.Exp,
                scale=-0.5), wait=[t_c], inc="sc")
            t_e = sq.do("vector", lambda e, c0=c0: e.tensor_mul(
                st[:, c0 + 6:c0 + 7], st[:, c0 + 6:c0 + 7],
                st[:, c0 + 5:c0 + 6]), wait=[t_d], inc="dv")
            t2 = sq.do("vector", lambda e, src_ap=src_ap, c0=c0:
                       e.tensor_scalar(
                           xn2[:, 0:D], src_ap,
                           st[:, c0 + 2:c0 + 3], st[:, c0 + 6:c0 + 7],
                           ALU.subtract, ALU.mult),
                       wait=[t_e, t_sq] + bufs["xn2"].war(), inc="dv")
            bufs["xn2"].wrote(t2)
            t_xn2.append(t2)
            tokt = None
            for c in range(8):
                tokt = sq.do("tensor", lambda e, i=i, c=c: e.transpose(
                    psT[:, c * 128:(c + 1) * 128],
                    xn2[:, c * 128:(c + 1) * 128], id_s[:]),
                    wait=([t2] + bufs["psT"].war()) if c == 0 else (),
                    inc="pe" if c == 7 else None)
            bufs["psT"].wrote(tokt)
            bufs["xn2"].read(tokt)
            ev_last = sq.do("scalar", lambda e, i=i: e.activation(
                xnT[:].rearrange("p (c t) -> p c t", t=TILE)
                [:, :, i * 128:(i + 1) * 128],
                psT[:], AF.Identity),
                wait=[tokt] + (bufs["xnT"].war() if i == 0 else []),
                inc="sc")
            bufs["psT"].read(ev_last)
            bufs["xnT"].wrote_more(ev_last)
        return xl, jj

    pending = None
    tiles = [(s_i, j) for s_i in range(SEQ_PER_CORE) for j in range(NT)]
    for ti_idx, (s_i, j) in enumerate(tiles):
        r0 = s_i * T
        jt0 = j * TILE
        jj = (s_i * NT + j) % 2
        xo = jj * NSL * D
        xbuf = bufs["xt%d" % jj]
        if pending is None:
            pending = emit_load_lnt(s_i, j)
        xl, _jj = pending
        t_xnT = sq.cur("sc")

        for c in range(8 if _P1LVL >= 3 else 0):
            tokm = None
            for dch in range(8):
                tokm = sq.do("tensor", lambda e, c=c, dch=dch: e.matmul(
                    psQ[:], qkvw[:, dch * H3 + c * 128:
                                 dch * H3 + (c + 1) * 128],
                    xnT[:, dch * TILE:(dch + 1) * TILE],
                    start=(dch == 0), stop=(dch == 7)),
                    wait=([t_xnT] + wall + bufs["psQ"].war())
                    if dch == 0 else (),
                    inc="pe" if dch == 7 else None)
            bufs["psQ"].wrote(tokm)
            bufs["xnT"].read(tokm)
            tq = elu_chain(psQ[:], tokm, bufs["psQ"],
                           q_fm[:, c * TILE:(c + 1) * TILE],
                           bufs["q"].war() if c == 0 else [],
                           bcol[:, c:c + 1])
            bufs["q"].wrote_more(tq)

            tokg = None
            for dch in range(8):
                tokg = sq.do("tensor", lambda e, c=c, dch=dch: e.matmul(
                    psQ2[:], gatew[:, dch * D + c * 128:
                                   dch * D + (c + 1) * 128],
                    xnT[:, dch * TILE:(dch + 1) * TILE],
                    start=(dch == 0), stop=(dch == 7)),
                    wait=(bufs["psQ2"].war()) if dch == 0 else (),
                    inc="pe" if dch == 7 else None)
            bufs["psQ2"].wrote(tokg)
            bufs["xnT"].read(tokg)
            tg = sq.do("scalar", lambda e, c=c: e.activation(
                gate_t[:], psQ2[:], AF.Sigmoid, bias=bcol[:, 24 + c:25 + c]),
                wait=[tokg] + bufs["gate"].war(), inc="sc")
            bufs["gate"].wrote(tg)
            bufs["psQ2"].read(tg)

            tokk = None
            for dch in range(8):
                tokk = sq.do("tensor", lambda e, c=c, dch=dch: e.matmul(
                    psQ2[:], qkvw[:, dch * H3 + D + c * 128:
                                  dch * H3 + D + (c + 1) * 128],
                    xnT[:, dch * TILE:(dch + 1) * TILE],
                    start=(dch == 0), stop=(dch == 7)),
                    wait=(bufs["psQ2"].war()) if dch == 0 else (),
                    inc="pe" if dch == 7 else None)
            bufs["psQ2"].wrote(tokk)
            bufs["xnT"].read(tokk)
            tkg = sq.do("vector", lambda e, c=c: e.scalar_tensor_tensor(
                kg[:], psQ2[:], bcol[:, 8 + c:9 + c], gate_t[:],
                ALU.add, ALU.mult),
                wait=[tokk, tg] + bufs["kg"].war(), inc="dv")
            bufs["kg"].wrote(tkg)
            bufs["psQ2"].read(tkg)
            bufs["gate"].read(tkg)
            tk = elu_chain(kg[:], tkg, bufs["kg"],
                           k_fm[:, c * T + jt0: c * T + jt0 + TILE],
                           bufs["k"].war() if (c == 0 and j == 0) else [],
                           None)
            bufs["k"].wrote_more(tk)
        t_k_done = sq.cur("dv")

        tv = None
        for i in range(NSL if (_P1LVL >= 3 and "v" in _P1SUB) else 0):
            tokv = None
            for half in range(2):
                f0 = 2 * D + half * TILE
                sq.do("tensor", lambda e, f0=f0, half=half: e.matmul(
                    psV[:, half * TILE:(half + 1) * TILE],
                    ones_row[:, 0:128], brow_v[:, f0 - 2 * D:f0 - 2 * D + TILE],
                    start=True, stop=False),
                    wait=([t_xnT] + bufs["psV"].war()) if half == 0 else ())
                for dch in range(8):
                    tokv = sq.do("tensor",
                                 lambda e, i=i, f0=f0, dch=dch, half=half:
                                 e.matmul(
                                     psV[:, half * TILE:(half + 1) * TILE],
                                     xnT[:, dch * TILE + i * 128:
                                         dch * TILE + (i + 1) * 128],
                                     qkvw[:, dch * H3 + f0:
                                          dch * H3 + f0 + TILE],
                                     start=False, stop=(dch == 7)),
                                 inc="pe" if (dch == 7 and half == 1) else None)
            bufs["psV"].wrote(tokv)
            bufs["xnT"].read(tokv)
            tv = sq.do("scalar", lambda e, i=i, jt0=jt0: e.activation(
                v_tm[:, (jt0 // 128 + i) * D:(jt0 // 128 + i + 1) * D],
                psV[:], AF.Identity),
                wait=[tokv] + (bufs["v"].war() if (i == 0 and j == 0) else []),
                inc="sc")
            bufs["psV"].read(tv)
            bufs["v"].wrote_more(tv)
        t_v_done = tv

        if ti_idx + 1 < len(tiles):
            pending = emit_load_lnt(*tiles[ti_idx + 1])
        else:
            pending = None

        n_stile = (4 * (j + 1)) if _P1LVL >= 4 else 0
        a_toks = []
        for stp in range(n_stile):
            sk0 = stp * 128
            ps_a = psQ if stp % 2 == 0 else psQ2
            pbuf = bufs["psQ" if stp % 2 == 0 else "psQ2"]
            toka = None
            for dch in range(8):
                toka = sq.do("tensor",
                             lambda e, sk0=sk0, dch=dch, ps_a=ps_a:
                             e.matmul(
                    ps_a[:], k_fm[:, dch * T + sk0: dch * T + sk0 + 128],
                    q_fm[:, dch * TILE:(dch + 1) * TILE],
                    start=(dch == 0), stop=(dch == 7)),
                    wait=([t_k_done] + pbuf.war() +
                          (bufs["a_sb"].war() if stp == 0 else []))
                    if dch == 0 else (),
                    inc="pe" if dch == 7 else None)
            pbuf.wrote(toka)
            bufs["k"].read(toka)
            bufs["q"].read(toka)
            if stp >= 4 * j:
                o = stp - 4 * j
                ta = sq.do("vector", lambda e, stp=stp, o=o, ps_a=ps_a:
                           e.tensor_mul(
                    a_sb[:, stp * TILE:(stp + 1) * TILE], ps_a[:],
                    maskt_s[:, o * TILE:(o + 1) * TILE]),
                    wait=[toka], inc="dv")
            else:
                ta = sq.do("scalar", lambda e, stp=stp, ps_a=ps_a:
                           e.activation(
                    a_sb[:, stp * TILE:(stp + 1) * TILE], ps_a[:],
                    AF.Identity),
                    wait=[toka], inc="sc")
            pbuf.read(ta)
            bufs["a_sb"].wrote_more(ta)
            a_toks.append(ta)

        tokd = None
        for i in range(NSL if _P1LVL >= 5 else 0):
            for stp in range(n_stile):
                tokd = sq.do("tensor", lambda e, i=i, stp=stp, ns=n_stile:
                    e.matmul(
                    psD[:, i:i + 1],
                    a_sb[:, stp * TILE + i * 128: stp * TILE + (i + 1) * 128],
                    ones_col[:, :],
                    start=(stp == 0 and i == 0),
                    stop=(stp == ns - 1 and i == NSL - 1)),
                    wait=([a_toks[stp]] +
                          (bufs["psD"].war() if (i == 0 and stp == 0) else [])),
                    inc="pe" if (stp == n_stile - 1 and i == NSL - 1)
                    else None)
        bufs["psD"].wrote(tokd)
        if _P1LVL < 5:
            ts_ = sq.do("gpsimd", lambda e, r0=r0, jt0=jt0, xo=xo:
                        e.dma_start(
                ap_rows(xout_d, r0 + jt0, NSL), xt[:, xo:xo + NSL * D]),
                wait=[xl] + xbuf.war(), inc="st", dma=True)
            xbuf.read(ts_)
            continue
        tden = sq.do("vector", lambda e: e.tensor_scalar(
            den[:, :], psD[:, :], DEN_EPS, None, ALU.add),
            wait=[tokd] + bufs["den"].war(), inc="dv")
        bufs["den"].wrote(tden)
        trc = sq.do("vector", lambda e: e.reciprocal(
            drec[:, :], den[:, :]),
            wait=[tden] + bufs["drec"].war(), inc="dv")
        bufs["den"].read(trc)
        bufs["psD"].read(trc)
        bufs["drec"].wrote(trc)

        ty = None
        for i in range(NSL):
            toko = None
            for stp in range(n_stile):
                w0 = [a_toks[stp]]
                if stp == 0:
                    w0 += bufs["psO"].war() + [t_v_done]
                for half in range(2):
                    toko = sq.do("tensor",
                          lambda e, i=i, stp=stp, half=half, ns=n_stile:
                          e.matmul(
                              psO[:, half * TILE:(half + 1) * TILE],
                              a_sb[:, stp * TILE + i * 128:
                                   stp * TILE + (i + 1) * 128],
                              v_tm[:, stp * D + half * TILE:
                                   stp * D + (half + 1) * TILE],
                              start=(stp == 0), stop=(stp == ns - 1)),
                          wait=w0 if half == 0 else (),
                          inc="pe" if (stp == n_stile - 1 and half == 1)
                          else None)
                    w0 = []
            bufs["psO"].wrote(toko)
            bufs["a_sb"].read(toko)
            bufs["v"].read(toko)
            for half in range(2):
                ty = sq.do("vector", lambda e, i=i, half=half, xo=xo:
                           e.scalar_tensor_tensor(
                               xt[:, xo + i * D + half * TILE:
                                  xo + i * D + half * TILE + TILE],
                               psO[:, half * TILE:(half + 1) * TILE],
                               drec[:, i:i + 1],
                               xt[:, xo + i * D + half * TILE:
                                  xo + i * D + half * TILE + TILE],
                               ALU.mult, ALU.add),
                           wait=[trc, toko], inc="dv")
                bufs["psO"].read(ty)
            bufs["drec"].read(ty)
            xbuf.wrote_more(ty)

        ts_ = sq.do("gpsimd", lambda e, r0=r0, jt0=jt0, xo=xo:
                    e.dma_start(
            ap_rows(xout_d, r0 + jt0, NSL), xt[:, xo:xo + NSL * D]),
            wait=[ty], inc="st", dma=True)
        xbuf.read(ts_)

    sq.do("gpsimd", lambda e: e.nop(), wait=[sq.cur("st")])
    sq.build()
    es.close()


def ffn_phase(nc, g_es, name, xin_d, xout_d, wgT_d, wuT_d, wdT_d, C):
    es = ExitStack()
    sq = Sq(nc, g_es, name)
    s_ = lambda n, s, d: es.enter_context(nc.sbuf_tensor(name + "_" + n, s, d))
    p_ = lambda n, s, d: es.enter_context(nc.psum_tensor(name + "_" + n, s, d))
    id_s, eps_ln = C["id_s"], C["eps_ln"]
    R = 1.0 / 127.0

    wgT = s_("wgT", [128, 8 * FFN_H], BF16)
    wuT = s_("wuT", [128, 8 * FFN_H], BF16)
    wdT = s_("wdT", [128, 24 * D], BF16)
    xt = s_("xt", [128, NSL * D], F32)
    xn = s_("xn", [128, D], BF16)
    st = s_("st", [128, 8 * NSL], F32)
    xnT = s_("xnT", [128, 8 * TILE], BF16)
    sil = s_("sil", [128, TILE], BF16)
    h_fm = s_("h", [128, 24 * TILE], BF16)
    xr = s_("xr", [128, 2 * TILE], F32)

    psT = p_("psT", [128, 1024], BF16)
    psG = p_("psG", [128, TILE], F32)
    psU = p_("psU", [128, TILE], F32)
    psY = p_("psY", [128, 2 * TILE], F32)

    w1 = sq.do("sync", lambda e: e.dma_start(wgT[:], ap_cpf(wgT_d, 8, FFN_H)),
               inc="wl", dma=True)
    w2 = sq.do("sync", lambda e: e.dma_start(wuT[:], ap_cpf(wuT_d, 8, FFN_H)),
               inc="wl", dma=True)
    w3 = sq.do("sync", lambda e: e.dma_start(wdT[:], ap_cpf(wdT_d, 24, D)),
               inc="wl", dma=True)
    wall = [w1, w2, w3]

    bufs = {n: Buf() for n in ["xt", "xn", "xnT", "st", "sil", "h",
                               "xr0", "xr1", "psT", "psG", "psU",
                               "psY0", "psY1"]}

    def emit_load_lnt(ti):
        r0 = ti * TILE
        xl = sq.do("sync", lambda e, r0=r0: e.dma_start(
            xt[:], ap_rows(xin_d, r0, NSL)),
            wait=bufs["xt"].war(), inc="xl", dma=True)
        bufs["xt"].wrote(xl)


        t_xn = []
        for i in range(NSL):
            src_ap = xt[:, i * D:(i + 1) * D]
            _, t1 = emit_ln(
                sq, st, xn[:, 0:D], src_ap, xn[:, 0:D],
                i, [xl], bufs["st"].war(), bufs["xn"].war(), eps_ln)
            bufs["xn"].wrote(t1)
            bufs["xt"].read(t1)
            t_xn.append(t1)
            tokt = None
            for c in range(8):
                tokt = sq.do("tensor", lambda e, c=c: e.transpose(
                    psT[:, c * 128:(c + 1) * 128],
                    xn[:, c * 128:(c + 1) * 128], id_s[:]),
                    wait=([t1] + bufs["psT"].war()) if c == 0 else (),
                    inc="pe" if c == 7 else None)
            bufs["psT"].wrote(tokt)
            bufs["xn"].read(tokt)
            ev = sq.do("scalar", lambda e, i=i: e.activation(
                xnT[:].rearrange("p (c t) -> p c t", t=TILE)
                [:, :, i * 128:(i + 1) * 128],
                psT[:], AF.Identity),
                wait=[tokt] + (bufs["xnT"].war() if i == 0 else []),
                inc="sc")
            bufs["psT"].read(ev)
            bufs["xnT"].wrote_more(ev)
        t_xnT = sq.cur("sc")

        return xl, t_xnT

    pending = None
    NTILES = ROWS // TILE
    for ti in range(NTILES):
        r0 = ti * TILE
        if pending is None:
            pending = emit_load_lnt(ti)
        xl, t_xnT = pending
        for hc in range(24):
            tokg = None
            for dch in range(8):
                tokg = sq.do("tensor", lambda e, hc=hc, dch=dch: e.matmul(
                    psG[:], wgT[:, dch * FFN_H + hc * 128:
                                dch * FFN_H + (hc + 1) * 128],
                    xnT[:, dch * TILE:(dch + 1) * TILE],
                    start=(dch == 0), stop=(dch == 7)),
                    wait=([t_xnT] + wall + bufs["psG"].war()) if dch == 0 else (),
                    inc="pe" if dch == 7 else None)
            bufs["psG"].wrote(tokg)
            bufs["xnT"].read(tokg)
            toku = None
            for dch in range(8):
                toku = sq.do("tensor", lambda e, hc=hc, dch=dch: e.matmul(
                    psU[:], wuT[:, dch * FFN_H + hc * 128:
                                dch * FFN_H + (hc + 1) * 128],
                    xnT[:, dch * TILE:(dch + 1) * TILE],
                    start=(dch == 0), stop=(dch == 7)),
                    wait=(bufs["psU"].war()) if dch == 0 else (),
                    inc="pe" if dch == 7 else None)
            bufs["psU"].wrote(toku)
            bufs["xnT"].read(toku)
            tsil = sq.do("scalar", lambda e: e.activation(
                sil[:], psG[:], AF.Silu, scale=R),
                wait=[tokg] + bufs["sil"].war(), inc="sc")
            bufs["sil"].wrote(tsil)
            bufs["psG"].read(tsil)
            th = sq.do("vector", lambda e, hc=hc: e.scalar_tensor_tensor(
                h_fm[:, hc * TILE:(hc + 1) * TILE], psU[:], R, sil[:],
                ALU.mult, ALU.mult),
                wait=[toku, tsil] + (bufs["h"].war() if hc == 0 else []),
                inc="dv")
            bufs["psU"].read(th)
            bufs["sil"].read(th)
            bufs["h"].wrote_more(th)
        t_h = sq.cur("dv")

        if ti + 1 < NTILES:
            pending = emit_load_lnt(ti + 1)
        else:
            pending = None

        for ch in range(2 * NSL):
            i, half = ch // 2, ch % 2
            sl = ch % 2
            xb = bufs["xr%d" % sl]
            tokr = sq.do("gpsimd", lambda e, r0=r0, i=i, half=half, sl=sl:
                         e.dma_start(
                             xr[:, sl * TILE:(sl + 1) * TILE],
                             bass.AP(xin_d, (r0 + i * 128) * D + half * TILE,
                                     [[D, 128], [1, TILE]])),
                         wait=xb.war(), inc="xr", dma=True)
            xb.wrote(tokr)
            yb = bufs["psY%d" % sl]
            toky = None
            for hc in range(24):
                toky = sq.do("tensor", lambda e, i=i, half=half, hc=hc, sl=sl:
                             e.matmul(
                                 psY[:, sl * TILE:(sl + 1) * TILE],
                                 h_fm[:, hc * TILE + i * 128:
                                      hc * TILE + (i + 1) * 128],
                                 wdT[:, hc * D + half * TILE:
                                     hc * D + half * TILE + TILE],
                                 start=(hc == 0), stop=(hc == 23)),
                             wait=([t_h] + yb.war()) if hc == 0 else (),
                             inc="pe" if hc == 23 else None)
            yb.wrote(toky)
            bufs["h"].read(toky)
            ty = sq.do("vector", lambda e, sl=sl: e.scalar_tensor_tensor(
                xr[:, sl * TILE:(sl + 1) * TILE],
                psY[:, sl * TILE:(sl + 1) * TILE], R,
                xr[:, sl * TILE:(sl + 1) * TILE],
                ALU.mult, ALU.add),
                wait=[toky, tokr], inc="dv")
            yb.read(ty)
            xb.wrote_more(ty)
            ts_ = sq.do("gpsimd", lambda e, r0=r0, i=i, half=half, sl=sl:
                        e.dma_start(
                            bass.AP(xout_d, (r0 + i * 128) * D + half * TILE,
                                    [[D, 128], [1, TILE]]),
                            xr[:, sl * TILE:(sl + 1) * TILE]),
                        wait=[ty], inc="st_o", dma=True)
            xb.read(ts_)

    sq.do("gpsimd", lambda e: e.nop(), wait=[sq.cur("st_o")])
    sq.build()
    es.close()


def attn_b_phase(nc, g_es, xin_d, xout_d, wattn_d, battn_d, battnpc_d,
                 laattn_d, lbattn_d,
                 wproj_d, bproj_d, laproj_d, lbproj_d, C):
    es = ExitStack()
    sq = Sq(nc, g_es, "p3")
    s_ = lambda n, s, d: es.enter_context(nc.sbuf_tensor("p3_" + n, s, d))
    p_ = lambda n, s, d: es.enter_context(nc.psum_tensor("p3_" + n, s, d))
    id_s, maskt_s = C["id_s"], C["maskt_s"]
    ones_row, ones_col, eps_ln = C["ones_row"], C["ones_col"], C["eps_ln"]
    kvs_s, kvz_s, kvri, kvz5 = C["kvs_s"], C["kvz_s"], C["kvri"], C["kvz5"]
    sel2 = C["sel2"]

    wattn = s_("wattn", [128, 8 * H3], BF16)
    wproj = s_("wproj", [128, 8 * D], BF16)
    la_sb = s_("la", [128, 8 * LORA_R], BF16)
    lb_sb = s_("lb", [LORA_R, H3], BF16)
    lap_sb = s_("lap", [128, 8 * LORA_R], BF16)
    lbp_sb = s_("lbp", [LORA_R, D], BF16)
    battn = s_("battn", [1, D], BF16)
    battnpc = s_("battnpc", [128, 16], F32)
    bq8 = s_("bq8", [128, 8], F32)
    bkz = s_("bkz", [128, 8], F32)
    bproj = s_("bproj", [1, D], BF16)
    xt = s_("xt", [128, NSL * D], F32)
    xr = s_("xr", [128, 2 * TILE], F32)
    xn = s_("xn", [128, D], BF16)
    st = s_("st", [128, 8 * NSL], F32)
    xnT = s_("xnT", [128, 8 * TILE], BF16)
    lm = s_("lm", [LORA_R, TILE], BF16)
    lmp = s_("lmp", [LORA_R, TILE], BF16)
    q_fm = s_("q", [128, 8 * TILE], BF16)
    k_fm = s_("k", [128, 8 * T], BF16)
    v_tm = s_("v", [128, 16 * D], BF16)
    negc = s_("negc", [128, N_HEAD], F32)
    l1m = s_("l1m", [2, 1], F32)
    l1mb = s_("l1mb", [2, 1], BF16)
    kbneg = s_("kbneg", [128, 1], F32)
    fq1 = s_("fq1", [128, TILE], F32)
    fqi = s_("fqi", [128, TILE], I32)
    fqr = s_("fqr", [128, TILE], BF16)
    fqg = s_("fqg", [128, TILE], BF16)
    absq = fqg
    recip = fq1
    ptA = s_("ptA", [128, 2 * TILE], BF16)
    ptB = s_("ptB", [128, 2 * TILE], BF16)
    out_fm = s_("outfm", [128, 8 * TILE], BF16)
    onesM = s_("onesM", [128, HD], BF16)

    psT = p_("psT", [128, 1024], BF16)
    psA = p_("psA", [128, 2 * TILE], F32)
    psOA = p_("psOA", [128, TILE], F32)
    psOB = p_("psOB", [128, TILE], F32)
    psDA = p_("psDA", [128, TILE], F32)
    psDB = p_("psDB", [128, TILE], F32)
    psY = p_("psY", [128, TILE], F32)
    psLM = psY
    psS = psA

    wl = []
    for dst, src_ap in (
        (wattn, ap_cpf(wattn_d, 8, H3)),
        (wproj, ap_cpf(wproj_d, 8, D)),
        (la_sb, ap_cpf(laattn_d, 8, LORA_R)),
        (lap_sb, ap_cpf(laproj_d, 8, LORA_R)),
    ):
        wl.append(sq.do("gpsimd", lambda e, dst=dst, src_ap=src_ap:
                        e.dma_start(dst[:], src_ap), inc="wl", dma=True))
    for dst, src in ((lb_sb, lbattn_d), (lbp_sb, lbproj_d),
                     (battn, battn_d), (bproj, bproj_d)):
        wl.append(sq.do("gpsimd", lambda e, dst=dst, src=src:
                        e.dma_start(dst[:], src[:, :]), inc="wl", dma=True))
    wl.append(sq.do("sync", lambda e: e.dma_start(
        battnpc[:], battnpc_d[:, :]), inc="wlb", dma=True))
    wall = [wl[-2], wl[-1]]

    sq.do("vector", lambda e: e.memset(onesM[:], 1.0), inc="dv")
    t1 = sq.do("vector", lambda e: e.tensor_scalar(
        kbneg[:], kvz_s[:], -1.0, 255.0, ALU.mult, ALU.add), inc="dv")
    t2 = sq.do("vector", lambda e: e.tensor_max(
        kbneg[:], kbneg[:], kvz_s[:]), wait=[t1], inc="dv")
    t3 = sq.do("vector", lambda e: e.tensor_mul(
        kbneg[:], kbneg[:], kvs_s[:]), wait=[t2], inc="dv")
    t_kb = sq.do("vector", lambda e: e.tensor_scalar(
        kbneg[:], kbneg[:], -1.0, None, ALU.mult), wait=[t3], inc="dv")
    sq.do("vector", lambda e: e.tensor_scalar(
        bq8[:], battnpc[:, 0:8], 0.125, None, ALU.mult),
        wait=[wl[-1]], inc="dv")
    t_bq = sq.do("vector", lambda e: e.tensor_scalar(
        bkz[:], battnpc[:, 8:16], kvri[:], kvz5[:], ALU.mult, ALU.add),
        wait=[wl[-1]], inc="dv")
    t_kb = t_bq

    bufs = {n: Buf() for n in
            ["xt", "xr0", "xr1", "xn", "xnT", "st", "lm", "lmp", "q", "k", "v",
             "negc",
             "l1m", "l1mb", "fq1", "fqi", "fqr", "fqg", "outfm",
             "ptA0", "ptA1", "ptB0", "ptB1",
             "psT", "psA0", "psA1", "psOA", "psOB", "psDA", "psDB", "psY"]}

    def fq_chain(src_ap, src_tok, src_buf, dst_ap, dst_war, zb=None):
        b1, bi, bf, bg = bufs["fq1"], bufs["fqi"], bufs["fqr"], bufs["fqg"]
        zb_ap = kvz5[:] if zb is None else zb
        ta = sq.do("vector", lambda e, zb_ap=zb_ap: e.tensor_scalar(
            fq1[:], src_ap, kvri[:], zb_ap, ALU.mult, ALU.add),
            wait=[src_tok, t_kb] + b1.war(), inc="dv")
        b1.wrote(ta)
        src_buf.read(ta)
        tb = sq.do("vector", lambda e: e.tensor_scalar(
            fq1[:], fq1[:], 0.5, 255.5, ALU.max, ALU.min),
            wait=[ta], inc="dv")
        b1.wrote_more(tb)
        tc = sq.do("scalar", lambda e: e.activation(
            fqi[:], fq1[:], AF.Identity),
            wait=[tb] + bi.war(), inc="sc")
        bi.wrote(tc)
        b1.read(tc)
        td = sq.do("scalar", lambda e: e.activation(
            fqr[:], fqi[:], AF.Identity),
            wait=[tc] + bf.war(), inc="sc")
        bf.wrote(td)
        bi.read(td)
        te = sq.do("vector", lambda e: e.tensor_tensor(
            fqg[:], fqr[:], fq1[:], ALU.is_gt),
            wait=[td] + bg.war(), inc="dv")
        bg.wrote(te)
        b1.read(te)
        bf.read(te)
        tf = sq.do("vector", lambda e: e.tensor_sub(
            fqr[:], fqr[:], fqg[:]), wait=[te], inc="dv")
        bf.wrote_more(tf)
        bg.read(tf)
        tg = sq.do("vector", lambda e: e.tensor_scalar(
            dst_ap, fqr[:], kvz_s[:], kvs_s[:], ALU.subtract, ALU.mult),
            wait=[tf] + list(dst_war), inc="dv")
        bf.read(tg)
        return tg

    def emit_load_lnt(sxi, j):
        r0 = sxi * T
        jt0 = j * TILE
        xo = 0
        xbuf = bufs["xt"]
        xl = sq.do("sync", lambda e, r0=r0, jt0=jt0: e.dma_start(
            xt[:], ap_rows(xin_d, r0 + jt0, NSL)),
            wait=xbuf.war(), inc="xl", dma=True)
        xbuf.wrote(xl)



        t_xn = []
        for i in range(NSL):
            _, t1n = emit_ln(
                sq, st, xn[:, 0:D],
                xt[:, xo + i * D: xo + (i + 1) * D], xn[:, 0:D],
                i, [xl], bufs["st"].war(), bufs["xn"].war(), eps_ln)
            bufs["xn"].wrote(t1n)
            t_xn.append(t1n)
            tokt = None
            for c in range(8):
                tokt = sq.do("tensor", lambda e, c=c: e.transpose(
                psT[:, c * 128:(c + 1) * 128],
                xn[:, c * 128:(c + 1) * 128], id_s[:]),
                wait=([t1n] + bufs["psT"].war()) if c == 0 else (),
                inc="pe" if c == 7 else None)
            bufs["psT"].wrote(tokt)
            bufs["xn"].read(tokt)
            ev = sq.do("scalar", lambda e, i=i: e.activation(
                xnT[:].rearrange("p (c t) -> p c t", t=TILE)
                [:, :, i * 128:(i + 1) * 128],
                psT[:], AF.Identity),
                wait=[tokt] + (bufs["xnT"].war() if i == 0 else []),
                inc="sc")
            bufs["psT"].read(ev)
            bufs["xnT"].wrote_more(ev)
        bufs["st"].wrote(None)
        bufs["st"].read(t_xn[-1])
        xbuf.read(t_xn[-1])
        t_xnT = sq.cur("sc")

        return xl, t_xnT

    pending = None
    tiles = [(sxi, j) for sxi in range(SEQ_PER_CORE) for j in range(NT)]
    for ti_idx, (sxi, j) in enumerate(tiles):
        r0 = sxi * T
        jt0 = j * TILE
        xo = 0
        xbuf = bufs["xt"]
        if pending is None:
            pending = emit_load_lnt(sxi, j)
        xl, t_xnT = pending
        tokl = None
        for dch in range(8):
            tokl = sq.do("tensor", lambda e, dch=dch: e.matmul(
                psLM[0:LORA_R, :], la_sb[:, dch * LORA_R:(dch + 1) * LORA_R],
                xnT[:, dch * TILE:(dch + 1) * TILE],
                start=(dch == 0), stop=(dch == 7)),
                wait=([t_xnT] + wl + bufs["psY"].war()) if dch == 0 else (),
                inc="pe" if dch == 7 else None)
        bufs["psY"].wrote(tokl)
        bufs["xnT"].read(tokl)
        t_lm = sq.do("scalar", lambda e: e.activation(
            lm[:], psLM[0:LORA_R, :], AF.Identity, scale=LORA_SCALE),
            wait=[tokl] + bufs["lm"].war(), inc="sc")
        bufs["psY"].read(t_lm)
        bufs["lm"].wrote(t_lm)

        for part, base_f in (("q", 0), ("k", D)):
            for c in range(8):
                f0 = base_f + c * 128
                for dch in range(8):
                    sq.do("tensor", lambda e, f0=f0, dch=dch: e.matmul(
                        psA[:, 0:TILE],
                        wattn[:, dch * H3 + f0: dch * H3 + f0 + 128],
                        xnT[:, dch * TILE:(dch + 1) * TILE],
                        start=(dch == 0), stop=False),
                        wait=(bufs["psA0"].war()) if dch == 0 else ())
                tokm = sq.do("tensor", lambda e, f0=f0: e.matmul(
                    psA[:, 0:TILE], lb_sb[:, f0:f0 + 128], lm[:],
                    start=False, stop=True),
                    wait=[t_lm], inc="pe")
                bufs["psA0"].wrote(tokm)
                bufs["xnT"].read(tokm)
                bufs["lm"].read(tokm)
                if part == "q":
                    tq = sq.do("scalar", lambda e, c=c: e.activation(
                        q_fm[:, c * TILE:(c + 1) * TILE], psA[:, 0:TILE],
                        AF.Identity, scale=0.125, bias=bq8[:, c:c + 1]),
                        wait=[tokm, t_bq] + (bufs["q"].war() if c == 0
                                             else []),
                        inc="sc")
                    bufs["q"].wrote_more(tq)
                    ta = sq.do("scalar", lambda e, c=c: e.activation(
                        absq[:], psA[:, 0:TILE], AF.Abs, scale=0.125,
                        bias=bq8[:, c:c + 1]),
                        wait=[tokm, t_bq] + bufs["fqg"].war(), inc="sc")
                    bufs["psA0"].read(tq)
                    bufs["psA0"].read(ta)
                    bufs["fqg"].wrote(ta)
                    tl1 = sq.do("tensor", lambda e: e.matmul(
                        psS[0:2, TILE:2 * TILE], C["ones2"][:, :], absq[:],
                        start=True, stop=True),
                        wait=[ta] + bufs["psA1"].war(), inc="pe")
                    bufs["psA1"].wrote(tl1)
                    bufs["fqg"].read(tl1)
                    tmx = sq.do("vector", lambda e: e.tensor_reduce(
                        l1m[:], psS[0:2, TILE:2 * TILE], AX.X, ALU.max),
                        wait=[tl1] + bufs["l1m"].war(), inc="dv")
                    bufs["l1m"].wrote(tmx)
                    bufs["psA1"].read(tmx)
                    tcb = sq.do("vector", lambda e: e.tensor_copy(
                        l1mb[:], l1m[:]),
                        wait=[tmx] + bufs["l1mb"].war(), inc="dv")
                    bufs["l1mb"].wrote(tcb)
                    bufs["l1m"].read(tcb)
                    tb1 = sq.do("tensor", lambda e: e.matmul(
                        psDA[:, 0:1], sel2[:, 0:128], l1mb[:],
                        start=True, stop=False),
                        wait=[tcb] + bufs["psDA"].war())
                    tb2 = sq.do("tensor", lambda e: e.matmul(
                        psDA[:, 1:2], sel2[:, 128:256], l1mb[:],
                        start=False, stop=True),
                        inc="pe")
                    bufs["psDA"].wrote(tb2)
                    bufs["l1mb"].read(tb2)
                    tnc = sq.do("vector", lambda e, c=c: e.tensor_scalar(
                        negc[:, 2 * c:2 * c + 2], psDA[:, 0:2], kbneg[:],
                        None, ALU.mult),
                        wait=[tb2] + (bufs["negc"].war() if c == 0 else []),
                        inc="dv")
                    bufs["psDA"].read(tnc)
                    bufs["negc"].wrote_more(tnc)
                else:
                    tk = fq_chain(psA[:, 0:TILE], tokm, bufs["psA0"],
                                  k_fm[:, c * T + jt0: c * T + jt0 + TILE],
                                  bufs["k"].war() if (c == 0 and j == 0)
                                  else [], zb=bkz[:, c:c + 1])
                    bufs["k"].wrote_more(tk)

        for i in range(NSL):
            for half in range(2):
                f0 = 2 * D + half * TILE
                sq.do("tensor", lambda e, f0=f0, half=half: e.matmul(
                    psA[:, half * TILE:(half + 1) * TILE],
                    ones_row[:, 0:128],
                    battn[:, f0 - 2 * D:f0 - 2 * D + TILE],
                    start=True, stop=False),
                    wait=(bufs["psA0"].war() + bufs["psA1"].war())
                    if half == 0 else ())
                for dch in range(8):
                    sq.do("tensor",
                          lambda e, i=i, f0=f0, dch=dch, half=half:
                          e.matmul(
                              psA[:, half * TILE:(half + 1) * TILE],
                              xnT[:, dch * TILE + i * 128:
                                  dch * TILE + (i + 1) * 128],
                              wattn[:, dch * H3 + f0: dch * H3 + f0 + TILE],
                              start=False, stop=False))
                tokv = sq.do("tensor",
                             lambda e, i=i, f0=f0, half=half: e.matmul(
                                 psA[:, half * TILE:(half + 1) * TILE],
                                 lm[:, i * 128:(i + 1) * 128],
                                 lb_sb[:, f0:f0 + TILE],
                                 start=False, stop=True),
                             inc="pe")
            bufs["psA0"].wrote(tokv)
            bufs["psA1"].wrote(tokv)
            bufs["xnT"].read(tokv)
            bufs["lm"].read(tokv)
            for half in range(2):
                tvq = fq_chain(
                    psA[:, half * TILE:(half + 1) * TILE], tokv,
                    bufs["psA0" if half == 0 else "psA1"],
                    v_tm[:, (jt0 // 128 + i) * D + half * TILE:
                         (jt0 // 128 + i) * D + (half + 1) * TILE],
                    bufs["v"].war() if (i == 0 and j == 0 and half == 0)
                    else [])
                bufs["v"].wrote_more(tvq)
        t_v_done = sq.cur("dv")

        if ti_idx + 1 < len(tiles):
            pending = emit_load_lnt(*tiles[ti_idx + 1])
        else:
            pending = None

        n_stile = 4 * (j + 1)
        for pair in range(8):
            hA, hB = 2 * pair, 2 * pair + 1
            last_av = None
            ptoks = {}

            def emit_S(stp):
                pb = stp % 2
                for (hh, off, pt, key, psS_h, psSb) in (
                        (hA, 0, ptA, "ptA", psA, "psA0"),
                        (hB, 64, ptB, "ptB", psA, "psA1")):
                    sl = 0 if psSb == "psA0" else TILE
                    tS = sq.do("tensor",
                               lambda e, off=off, stp=stp, sl=sl,
                               pair=pair: e.matmul(
                                   psA[:, sl:sl + TILE],
                                   k_fm[off:off + 64,
                                        pair * T + stp * 128:
                                        pair * T + stp * 128 + 128],
                                   q_fm[off:off + 64,
                                        pair * TILE:(pair + 1) * TILE],
                                   start=True, stop=True),
                               wait=([sq.cur("dv"), sq.cur("sc")]
                                     if (stp == 0 and pair == 0) else []) +
                               bufs[psSb].war(),
                               inc="pe")
                    bufs[psSb].wrote(tS)
                    bufs["k"].read(tS)
                    bufs["q"].read(tS)
                    pb2 = stp % 2
                    tE = sq.do("scalar",
                               lambda e, hh=hh, pt=pt, pb=pb2, sl=sl:
                               e.activation(
                                   pt[:, pb * TILE:(pb + 1) * TILE],
                                   psA[:, sl:sl + TILE], AF.Exp,
                                   bias=negc[:, hh:hh + 1]),
                               wait=[tS] + bufs[key + str(pb2)].war(),
                               inc="sc")
                    bufs[psSb].read(tE)
                    bufs["negc"].read(tE)
                    bufs[key + str(pb2)].wrote(tE)
                    tlast = tE
                    if stp >= 4 * j:
                        o = stp - 4 * j
                        tlast = sq.do("vector",
                                      lambda e, pt=pt, pb=pb2, o=o:
                                      e.tensor_mul(
                                          pt[:, pb * TILE:(pb + 1) * TILE],
                                          pt[:, pb * TILE:(pb + 1) * TILE],
                                          maskt_s[:, o * TILE:(o + 1) * TILE]),
                                      wait=[tE], inc="dv")
                        bufs[key + str(pb2)].wrote_more(tlast)
                    ptoks[(hh, stp)] = tlast

            def emit_AV(stp):
                nonlocal last_av
                pb = stp % 2
                wv = [ptoks[(hA, stp)], ptoks[(hB, stp)], t_v_done]
                if stp == 0:
                    wv += (bufs["psOA"].war() + bufs["psOB"].war() +
                           bufs["psDA"].war() + bufs["psDB"].war())
                for (hh, off, pt, key, tp, psO_h, psD_h) in (
                        (hA, 0, ptA, "ptA", (0, 0), psOA, psDA),
                        (hB, 64, ptB, "ptB", (0, 64), psOB, psDB)):
                    sq.do("tensor",
                          lambda e, hh=hh, off=off, pt=pt, pb=pb,
                          stp=stp, ns=n_stile, tp=tp, psO_h=psO_h:
                          e.matmul(
                              psO_h[off:off + 64, :],
                              v_tm[:, stp * D + hh * HD:
                                   stp * D + hh * HD + HD],
                              pt[:, pb * TILE:(pb + 1) * TILE],
                              start=(stp == 0),
                              stop=(stp == ns - 1),
                              tile_position=tp),
                          wait=wv)
                    wv.clear()
                    tDN = sq.do("tensor",
                                lambda e, off=off, pt=pt, pb=pb,
                                stp=stp, ns=n_stile, tp=tp, psD_h=psD_h:
                                e.matmul(
                                    psD_h[off:off + 64, :],
                                    onesM[:, :],
                                    pt[:, pb * TILE:(pb + 1) * TILE],
                                    start=(stp == 0),
                                    stop=(stp == ns - 1),
                                    tile_position=tp),
                                inc="pe" if off == 64 else None)
                    if tDN is not None:
                        bufs["ptA" + str(pb)].read(tDN)
                        bufs["ptB" + str(pb)].read(tDN)
                        last_av = tDN

            emit_S(0)
            for stp in range(1, n_stile):
                emit_S(stp)
                emit_AV(stp - 1)
            emit_AV(n_stile - 1)
            bufs["psOA"].wrote(last_av)
            bufs["psOB"].wrote(last_av)
            bufs["psDA"].wrote(last_av)
            bufs["psDB"].wrote(last_av)
            tln = sq.do("scalar", lambda e: e.activation(
                recip[0:64, :], psDA[0:64, :], AF.Ln),
                wait=[last_av] + bufs["fq1"].war(), inc="sc")
            tln2 = sq.do("scalar", lambda e: e.activation(
                recip[64:128, :], psDB[64:128, :], AF.Ln),
                inc="sc")
            trc = sq.do("scalar", lambda e: e.activation(
                recip[:], recip[:], AF.Exp, scale=-1.0),
                wait=[tln2], inc="sc")
            bufs["fq1"].wrote(trc)
            bufs["psDA"].read(trc)
            bufs["psDB"].read(trc)
            to1 = sq.do("vector", lambda e, pair=pair: e.tensor_mul(
                out_fm[0:64, pair * TILE:(pair + 1) * TILE],
                psOA[0:64, :], recip[0:64, :]),
                wait=[trc] + (bufs["outfm"].war() if pair == 0 else []),
                inc="dv")
            to2 = sq.do("vector", lambda e, pair=pair: e.tensor_mul(
                out_fm[64:128, pair * TILE:(pair + 1) * TILE],
                psOB[64:128, :], recip[64:128, :]),
                wait=[trc], inc="dv")
            bufs["psOA"].read(to1)
            bufs["psOB"].read(to2)
            bufs["fq1"].read(to2)
            bufs["outfm"].wrote_more(to2)
        t_attn = sq.cur("dv")

        tokl2 = None
        for cc in range(8):
            tokl2 = sq.do("tensor", lambda e, cc=cc: e.matmul(
                psLM[0:LORA_R, :], lap_sb[:, cc * LORA_R:(cc + 1) * LORA_R],
                out_fm[:, cc * TILE:(cc + 1) * TILE],
                start=(cc == 0), stop=(cc == 7)),
                wait=([t_attn] + bufs["psY"].war()) if cc == 0 else (),
                inc="pe" if cc == 7 else None)
        bufs["psY"].wrote(tokl2)
        bufs["outfm"].read(tokl2)
        t_lmp = sq.do("scalar", lambda e: e.activation(
            lmp[:], psLM[0:LORA_R, :], AF.Identity, scale=LORA_SCALE),
            wait=[tokl2] + bufs["lmp"].war(), inc="sc")
        bufs["psY"].read(t_lmp)
        bufs["lmp"].wrote(t_lmp)

        ty = None
        for i in range(NSL):
            for half in range(2):
                f0 = half * TILE
                ch = 2 * i + half
                sl = ch % 2
                xb2 = bufs["xr%d" % sl]
                tokr = sq.do("gpsimd",
                             lambda e, r0=r0, jt0=jt0, i=i, half=half,
                             sl=sl: e.dma_start(
                                 xr[:, sl * TILE:(sl + 1) * TILE],
                                 bass.AP(xin_d,
                                         (r0 + jt0 + i * 128) * D +
                                         half * TILE,
                                         [[D, 128], [1, TILE]])),
                             wait=xb2.war(), inc="xr", dma=True)
                xb2.wrote(tokr)
                sq.do("tensor", lambda e, f0=f0: e.matmul(
                    psY[:], ones_row[:, 0:128], bproj[:, f0:f0 + TILE],
                    start=True, stop=False),
                    wait=[t_attn] + bufs["psY"].war())
                for cc in range(8):
                    sq.do("tensor", lambda e, i=i, f0=f0, cc=cc: e.matmul(
                        psY[:],
                        out_fm[:, cc * TILE + i * 128:
                               cc * TILE + (i + 1) * 128],
                        wproj[:, cc * D + f0: cc * D + f0 + TILE],
                        start=False, stop=False))
                toky = sq.do("tensor", lambda e, i=i, f0=f0: e.matmul(
                    psY[:], lmp[:, i * 128:(i + 1) * 128],
                    lbp_sb[:, f0:f0 + TILE],
                    start=False, stop=True),
                    wait=[t_lmp], inc="pe")
                bufs["psY"].wrote(toky)
                bufs["outfm"].read(toky)
                bufs["lmp"].read(toky)
                ty = sq.do("vector", lambda e, sl=sl:
                           e.tensor_add(
                               xr[:, sl * TILE:(sl + 1) * TILE],
                               psY[:],
                               xr[:, sl * TILE:(sl + 1) * TILE]),
                           wait=[toky, tokr], inc="dv")
                bufs["psY"].read(ty)
                xb2.wrote_more(ty)
                ts_ = sq.do("gpsimd",
                            lambda e, r0=r0, jt0=jt0, i=i, half=half,
                            sl=sl: e.dma_start(
                                bass.AP(xout_d,
                                        (r0 + jt0 + i * 128) * D +
                                        half * TILE,
                                        [[D, 128], [1, TILE]]),
                                xr[:, sl * TILE:(sl + 1) * TILE]),
                            wait=[ty], inc="st_o", dma=True)
                xb2.read(ts_)

    sq.do("gpsimd", lambda e: e.nop(), wait=[sq.cur("st_o")])
    sq.build()
    es.close()


def _copy_phase(nc, g_es, name, xin_d, xout_d):
    es = ExitStack()
    sq = Sq(nc, g_es, name)
    buf = es.enter_context(nc.sbuf_tensor(name + "_buf", [128, 2 * 4096], F32))
    b0, b1 = Buf(), Buf()
    for i in range(ROWS // 128):
        bb = b0 if i % 2 == 0 else b1
        ld = sq.do("sync", lambda e, i=i: e.dma_start(
            buf[:, (i % 2) * 4096:(i % 2) * 4096 + 1024],
            xin_d[i * 128:(i + 1) * 128, :]),
            wait=bb.war(), inc="ld", dma=True)
        bb.wrote(ld)
        st = sq.do("gpsimd", lambda e, i=i: e.dma_start(
            xout_d[i * 128:(i + 1) * 128, :],
            buf[:, (i % 2) * 4096:(i % 2) * 4096 + 1024]),
            wait=[ld], inc="st", dma=True)
        bb.read(st)
    sq.do("gpsimd", lambda e: e.nop(), wait=[sq.cur("st")])
    sq.build()
    es.close()


_NC_CACHE = {}


def _get_program():
    key = _PHASES
    if key not in _NC_CACHE:
        _NC_CACHE[key] = build_program()
    return _NC_CACHE[key]


def _host_consts():
    ident = np.eye(128, dtype=ml_dtypes.bfloat16)
    m = np.zeros((128, 4 * TILE), dtype=ml_dtypes.bfloat16)
    t = np.arange(TILE)
    p = np.arange(128)[:, None]
    for oi, o in enumerate((0, 128, 256, 384)):
        m[:, oi * TILE:(oi + 1) * TILE] = (t[None, :] >= (o + p)).astype(
            ml_dtypes.bfloat16)
    sel2 = np.zeros((2, 256), dtype=ml_dtypes.bfloat16)
    sel2[0, 0:128] = 1
    sel2[1, 128:256] = 1
    ones2 = np.zeros((128, 2), dtype=ml_dtypes.bfloat16)
    ones2[0:64, 0] = 1
    ones2[64:128, 1] = 1
    return ident, m, sel2, ones2


def kernel(**inp):
    nc = _get_program()
    ident, maskt, sel2, ones2 = _host_consts()

    x = np.ascontiguousarray(inp["L0_attn_a_x"]).reshape(B * T, D)

    def row(v):
        return np.ascontiguousarray(np.asarray(v, np.float32).reshape(1, -1))

    def pcol(v, c):
        return np.ascontiguousarray(np.asarray(v, np.float32).reshape(c, 128).T)

    shared = dict(
        qkv_w=np.ascontiguousarray(inp["L0_attn_a_qkv_w"], dtype=np.float32),
        qkv_b=row(inp["L0_attn_a_qkv_b"]),
        qkv_b_pc=pcol(inp["L0_attn_a_qkv_b"], 24),
        gate_b_pc=pcol(inp["L0_attn_a_gate_b"], 8),
        gate_w=np.ascontiguousarray(inp["L0_attn_a_gate_w"], dtype=np.float32),
        gate_b=row(inp["L0_attn_a_gate_b"]),
        ln_g=pcol(inp["L0_attn_a_ln_g"], 8),
        ln_b=pcol(inp["L0_attn_a_ln_b"], 8),
        w_attn=np.ascontiguousarray(inp["L0_attn_b_w_attn"], dtype=np.float32),
        b_attn=row(np.asarray(inp["L0_attn_b_b_attn"], np.float32)[2 * D:3 * D]),
        b_attn_pc=pcol(np.asarray(inp["L0_attn_b_b_attn"], np.float32)[0:2 * D], 16),
        la_attn=np.ascontiguousarray(inp["L0_attn_b_la_attn"], dtype=np.float32),
        lb_attn=np.ascontiguousarray(inp["L0_attn_b_lb_attn"], dtype=np.float32),
        w_proj=np.ascontiguousarray(inp["L0_attn_b_w_proj"], dtype=np.float32),
        b_proj=row(inp["L0_attn_b_b_proj"]),
        la_proj=np.ascontiguousarray(inp["L0_attn_b_la_proj"], dtype=np.float32),
        lb_proj=np.ascontiguousarray(inp["L0_attn_b_lb_proj"], dtype=np.float32),
        kv_scale=np.ascontiguousarray(
            np.broadcast_to(np.asarray(inp["L0_attn_b_kv_scale"],
                                       np.float32).reshape(1, 1), (128, 1))),
        kv_zero=np.ascontiguousarray(
            np.broadcast_to(np.asarray(inp["L0_attn_b_kv_zero_point"],
                                       np.float32).reshape(1, 1), (128, 1))),
        ffn_w_gate=np.ascontiguousarray(inp["L0_ffn_w_gate"], dtype=np.int32),
        ffn_w_up=np.ascontiguousarray(inp["L0_ffn_w_up"], dtype=np.int32),
        ffn_w_down=np.ascontiguousarray(inp["L0_ffn_w_down"], dtype=np.int32),
        ident=ident,
        maskt=maskt,
        sel2=sel2,
        ones2=ones2,
    )

    in_maps = []
    for c in range(NCORE):
        m = dict(shared)
        m["x"] = np.ascontiguousarray(x[c * ROWS:(c + 1) * ROWS])
        in_maps.append(m)

    res = run_bass_kernel_spmd(nc, in_maps, core_ids=list(range(NCORE)))
    out = np.concatenate([r["out"] for r in res.results], axis=0)
    return out.reshape(B, T, D).astype(np.float32)


if __name__ == "__main__":
    print("smoke build only")
    build_program()
    print("build ok")
